# version 06
# baseline (speedup 1.0000x reference)
import numpy as np
import ml_dtypes
import concourse.bass as bass
import concourse.mybir as mybir
from concourse.bass_utils import run_bass_kernel_spmd

F32 = mybir.dt.float32
BF16 = mybir.dt.bfloat16
I32 = mybir.dt.int32
ALU = mybir.AluOpType
ACTF = mybir.ActivationFunctionType
AX = mybir.AxisListType.X

R = 2048
D = 2048
NB = 16
NT = 16
I8K = 8192
EPS = 1e-5
C2K = float(1.0 / np.sqrt(np.float64(2048.0)))
C8K = float(1.0 / np.sqrt(np.float64(8192.0)))
SCL8 = float(1.0 / np.sqrt(np.float64(8.0)))
EPS1 = float(EPS / C2K)
EPS2 = float(EPS / C8K)

QOFF, KOFF, VOFF = 0, 16, 32


def _hadamard128():
    h = np.ones((1, 1), np.float32)
    while h.shape[0] < 128:
        h = np.block([[h, h], [h, -h]])
    return h


class Sem:
    def __init__(self, h):
        self.h = h
        self.n = 0


class Builder:

    def __init__(self, nc):
        self.nc = nc
        self.prog = {k: [] for k in ("sync", "vector", "scalar", "gpsimd", "tensor")}

    def em(self, eng, fn):
        self.prog[eng].append(fn)

    def chain_op(self, eng, chain, fn_name, *a, **kw):
        n0 = chain.n
        import traceback
        site = traceback.extract_stack()[-3]

        def f(e):
            if n0:
                e.wait_ge(chain.h, n0)
            try:
                getattr(e, fn_name)(*a, **kw).then_inc(chain.h, 1)
            except Exception as ex:
                raise RuntimeError(
                    f"{fn_name} from {site.filename}:{site.lineno}: {ex}"
                ) from ex

        self.em(eng, f)
        chain.n += 1
        return chain.n

    def wait(self, eng, sem, val=None):
        v = sem.n if val is None else val
        if v > 0:
            self.em(eng, lambda e: e.wait_ge(sem.h, v))

    def pe(self, fn_name, *a, inc=None, **kw):

        def f(e):
            inst = getattr(e, fn_name)(*a, **kw)
            if inc is not None:
                inst.then_inc(inc.h, 1)

        self.em("tensor", f)
        if inc is not None:
            inc.n += 1
            return inc.n

    def dma(self, eng, out_ap, in_ap, sem, transpose=False):
        def f(e):
            if transpose:
                inst = e.dma_start_transpose(out_ap, in_ap)
            else:
                inst = e.dma_start(out=out_ap, in_=in_ap)
            inst.then_inc(sem.h, 16)

        self.em(eng, f)
        sem.n += 16
        return sem.n


def build_nc(dbg=False):
    nc = bass.Bass("TRN2", target_bir_lowering=False, debug=False)

    xs = nc.dram_tensor("xs", [R, D], F32, kind="ExternalInput").ap()
    xres = nc.dram_tensor("xres", [NT, 128, D], F32, kind="ExternalInput").ap()
    w_g = nc.dram_tensor("w_g", [I8K, D], F32, kind="ExternalInput").ap()
    w_u = nc.dram_tensor("w_u", [I8K, D], F32, kind="ExternalInput").ap()
    w_d = nc.dram_tensor("w_d", [D, I8K], F32, kind="ExternalInput").ap()
    c_bdqkv = nc.dram_tensor("c_bdqkv", [128, 384], F32, kind="ExternalInput").ap()
    c_bdproj = nc.dram_tensor("c_bdproj", [128, 128], F32, kind="ExternalInput").ap()
    c_h128 = nc.dram_tensor("c_h128", [128, 128], BF16, kind="ExternalInput").ap()
    c_ident = nc.dram_tensor("c_ident", [128, 128], BF16, kind="ExternalInput").ap()
    c_taps = nc.dram_tensor("c_taps", [128, 48], F32, kind="ExternalInput").ap()
    c_cb = nc.dram_tensor("c_cb", [128, 16], F32, kind="ExternalInput").ap()
    c_lnw = nc.dram_tensor("c_lnw", [128, 16], F32, kind="ExternalInput").ap()
    c_lnb = nc.dram_tensor("c_lnb", [128, 16], F32, kind="ExternalInput").ap()
    c_pb = nc.dram_tensor("c_pb", [128, 16], F32, kind="ExternalInput").ap()

    kind_i = "ExternalOutput" if dbg else "Internal"
    tern_g = nc.dram_tensor("tern_g", [I8K, D], BF16, kind=kind_i).ap()
    tern_u = nc.dram_tensor("tern_u", [I8K, D], BF16, kind="Internal").ap()
    tern_d = nc.dram_tensor("tern_d", [D, I8K], BF16, kind="Internal").ap()
    planes = nc.dram_tensor("planes", [NT, NB, 128, 128], BF16, kind=kind_i).ap()
    sdbg = nc.dram_tensor("sdbg", [128, 12], F32, kind=kind_i).ap() if dbg else None
    out = nc.dram_tensor("out", [NT, 128, D], F32, kind="ExternalOutput").ap()

    import contextlib
    with contextlib.ExitStack() as _stack:
        _E = _stack.enter_context
        s_bdqkv = _E(nc.sbuf_tensor("s_bdqkv", [128, 384], BF16))
        s_bdproj = _E(nc.sbuf_tensor("s_bdproj", [128, 128], BF16))
        s_h128 = _E(nc.sbuf_tensor("s_h128", [128, 128], BF16))
        s_ident = _E(nc.sbuf_tensor("s_ident", [128, 128], BF16))
        s_taps = _E(nc.sbuf_tensor("s_taps", [128, 48], F32))
        s_cb = _E(nc.sbuf_tensor("s_cb", [128, 16], F32))
        s_lnw = _E(nc.sbuf_tensor("s_lnw", [128, 16], F32))
        s_lnb = _E(nc.sbuf_tensor("s_lnb", [128, 16], F32))
        s_pb = _E(nc.sbuf_tensor("s_pb", [128, 16], F32))
        s_cst = _E(nc.sbuf_tensor("s_cst", [128, 4], F32))
        s_ones = _E(nc.sbuf_tensor("s_ones", [128, 128], BF16))
        s_onesf = _E(nc.sbuf_tensor("s_onesf", [128, 1], F32))
        s_onesg = _E(nc.sbuf_tensor("s_onesg", [128, 128], F32))
        s_svec = _E(nc.sbuf_tensor("s_svec", [128, 4], F32))
        s_thvec = _E(nc.sbuf_tensor("s_thvec", [128, 4], F32))
        s_wacc = _E(nc.sbuf_tensor("s_wacc", [128, 4], F32))
        s_stats = _E(nc.sbuf_tensor("s_stats", [128, 16], F32))
        s_mx2 = _E(nc.sbuf_tensor("s_mx2", [128, 4], F32))
        s_a2sd = _E(nc.sbuf_tensor("s_a2sd", [128, 4], F32))
        s_rows = _E(nc.sbuf_tensor("s_rows", [128, 2592], BF16))
        xf32a = _E(nc.sbuf_tensor("xf32a", [128, 2, 2048], F32))
        xf32b = _E(nc.sbuf_tensor("xf32b", [128, 2, 2048], F32))
        fmisc = _E(nc.sbuf_tensor("fmisc", [128, 160], F32))
        f128 = _E(nc.sbuf_tensor("f128", [128, 768], F32))
        mu1 = _E(nc.sbuf_tensor("mu1", [128, 1], F32))
        sg1 = _E(nc.sbuf_tensor("sg1", [128, 1], F32))
        qu = _E(nc.sbuf_tensor("qu", [128, 512], F32))
        qiq = _E(nc.sbuf_tensor("qiq", [128, 512], I32))
        qfq = _E(nc.sbuf_tensor("qfq", [128, 512], F32))
        mxf = _E(nc.sbuf_tensor("mxf", [128, 512], F32))
        A = _E(nc.sbuf_tensor("A", [128, 32768], BF16))
        B = _E(nc.sbuf_tensor("B", [128, 16384], BF16))
        C = _E(nc.sbuf_tensor("C", [128, 8192], BF16))
        DD = _E(nc.sbuf_tensor("DD", [128, 4096], BF16))
        x1b = _E(nc.sbuf_tensor("x1b", [128, 4, 2048], BF16))
        sm = _E(nc.sbuf_tensor("sm", [128, 4096], BF16))
        pbig = _E(nc.psum_tensor("pbig", [128, 2048], F32))
        pq = _E(nc.psum_tensor("pq", [128, 512], F32))
        pu = _E(nc.psum_tensor("pu", [128, 512], F32))
        ph = _E(nc.psum_tensor("ph", [128, 512], F32))
        pw = _E(nc.psum_tensor("pw", [128, 512], F32))
        chv_h = _E(nc.semaphore("chv"))
        cha_h = _E(nc.semaphore("cha"))
        chg_h = _E(nc.semaphore("chg"))
        pe_h = _E(nc.semaphore("pe0"))
        ldc_h = _E(nc.semaphore("ldc"))
        ldx0_h = _E(nc.semaphore("ldx0"))
        ldx1_h = _E(nc.semaphore("ldx1"))
        ldw0_h = _E(nc.semaphore("ldw0"))
        ldw1_h = _E(nc.semaphore("ldw1"))
        tw0_h = _E(nc.semaphore("tw0"))
        tw1_h = _E(nc.semaphore("tw1"))
        pls_h0 = _E(nc.semaphore("pls0"))
        pls_h1 = _E(nc.semaphore("pls1"))
        ldg0_h = _E(nc.semaphore("ldg0"))
        ldg1_h = _E(nc.semaphore("ldg1"))
        ldd0_h = _E(nc.semaphore("ldd0"))
        ldd1_h = _E(nc.semaphore("ldd1"))
        ldp0_h = _E(nc.semaphore("ldp0"))
        ldp1_h = _E(nc.semaphore("ldp1"))
        ldr0_h = _E(nc.semaphore("ldr0"))
        ldr1_h = _E(nc.semaphore("ldr1"))
        ost_h0 = _E(nc.semaphore("ost0"))
        ost_h1 = _E(nc.semaphore("ost1"))
        ost_h2 = _E(nc.semaphore("ost2"))
        ost_h3 = _E(nc.semaphore("ost3"))
        bar_h = _E(nc.semaphore("bar"))
        block = _E(nc.Block())
        bd = Builder(nc)
        chv, cha, chg, pe = Sem(chv_h), Sem(cha_h), Sem(chg_h), Sem(pe_h)
        ldc = Sem(ldc_h)
        ldx = [Sem(ldx0_h), Sem(ldx1_h)]
        ldw = [Sem(ldw0_h), Sem(ldw1_h)]
        tw = [Sem(tw0_h), Sem(tw1_h)]
        pls = [Sem(pls_h0), Sem(pls_h1)]
        ldg = [Sem(ldg0_h), Sem(ldg1_h)]
        ldd = [Sem(ldd0_h), Sem(ldd1_h)]
        ldp = [Sem(ldp0_h), Sem(ldp1_h)]
        ldr = [Sem(ldr0_h), Sem(ldr1_h)]
        ost = [Sem(ost_h0), Sem(ost_h1), Sem(ost_h2), Sem(ost_h3)]
        bar = Sem(bar_h)

        def v_op(fn, *a, **k):
            return bd.chain_op("vector", chv, fn, *a, **k)

        def a_op(fn, *a, **k):
            return bd.chain_op("scalar", cha, fn, *a, **k)

        def g_op(fn, *a, **k):
            return bd.chain_op("gpsimd", chg, fn, *a, **k)

        W = bd.wait
        PE = bd.pe
        DMA = bd.dma

        def a_slice(start, n):
            return A[:, start:start + n]

        prods = a_slice(0, 8192)
        qkvs = a_slice(8192, 6144)
        sbuf1 = a_slice(14336, 1024)
        sexp = a_slice(15360, 1024)
        oacc = a_slice(16384, 1024)
        olep = a_slice(17408, 1024)
        o2cm = a_slice(18432, 1024)
        ofull = a_slice(19456, 2048)
        t16a = a_slice(21504, 2048)
        b2q = a_slice(23552, 1024)
        b2k = a_slice(24576, 1024)
        b2v = a_slice(25600, 1024)
        ya = a_slice(26624, 2048)
        sqsb = a_slice(28672, 2048)
        tok2 = [B[:, 0:2048], B[:, 2048:4096]]
        tstr = [DD[:, 0:2048], DD[:, 2048:4096]]
        sgn1 = x1b[:, 3, :]
        xtg = C[:, 0:128]
        hbuf = A.rearrange("p (c n) -> p c n", n=512)
        xq1 = C.rearrange("p (c n) -> p c n", n=512)
        wgu = B.rearrange("p (s gu f o) -> p s gu f o", s=2, gu=2, f=16)
        wdt = [DD[:, 0:2048], DD[:, 2048:4096]]
        smax = f128[:, 0:128]
        smaxs = f128[:, 128:256]
        ssum = f128[:, 256:384]
        srcp = f128[:, 384:512]
        m16 = f128[:, 512:640]
        v16 = f128[:, 640:768]
        r16 = fmisc[:, 0:128]
        gsb = sm[:, 0:512]
        usb = sm[:, 512:1024]
        dqb = sm[:, 1024:1536]
        zs1 = sm[:, 1536:2048]
        zs2 = sm[:, 2048:2560]
        zzb = sm[:, 2560:3072]
        bca = sm[:, 3072:3584]
        bcb = sm[:, 3584:4096]
        bc2 = s_rows[:, 1536:2048]
        strows = s_rows[:, 0:1536]
        mxcast = s_rows[:, 2048:2560]
        stcast = s_rows[:, 2560:2592]

        DMA("sync", s_h128[:], c_h128, ldc)
        DMA("sync", s_ident[:], c_ident, ldc)
        DMA("sync", s_taps[:], c_taps, ldc)
        DMA("sync", s_cb[:], c_cb, ldc)
        DMA("sync", s_lnw[:], c_lnw, ldc)
        DMA("sync", s_lnb[:], c_lnb, ldc)
        DMA("sync", s_pb[:], c_pb, ldc)
        DMA("sync", xf32a[:, 0, 0:384], c_bdqkv, ldc)
        DMA("sync", xf32a[:, 0, 384:512], c_bdproj, ldc)
        W("vector", ldc)
        v_op("tensor_copy", s_bdqkv[:], xf32a[:, 0, 0:384])
        v_op("tensor_copy", s_bdproj[:], xf32a[:, 0, 384:512])
        v_op("memset", s_cst[:, 0:1], -0.5)
        v_op("memset", s_cst[:, 1:2], 0.5)
        v_op("memset", s_cst[:, 2:3], EPS)
        v_op("memset", s_ones[:], 1.0)
        v_op("memset", s_onesf[:], 1.0)
        v_op("memset", s_onesg[:], 1.0)
        v_op("memset", s_wacc[:], 0.0)
        consts_v = chv.n

        W("sync", chv, consts_v)
        DMA("sync", xf32a[:, 0, :], xs[0:128, :], ldx[0])

        wg_v = w_g.rearrange("(c p) d -> c p d", p=128)
        wu_v = w_u.rearrange("(c p) d -> c p d", p=128)
        def wd_v_f(ce):
            c0, e0 = ce // 4, ce % 4
            return w_d[128 * c0:128 * (c0 + 1), 2048 * e0:2048 * (e0 + 1)]
        tg_v = tern_g.rearrange("(c p) d -> c p d", p=128)
        tu_v = tern_u.rearrange("(c p) d -> c p d", p=128)
        def td_v_f(ce):
            c0, e0 = ce // 4, ce % 4
            return tern_d[128 * c0:128 * (c0 + 1), 2048 * e0:2048 * (e0 + 1)]

        scale_q = ([(wg_v[c], 0) for c in range(64)] +
                   [(wu_v[c], 1) for c in range(64)] +
                   [(wd_v_f(c), 2) for c in range(64)])
        tern_q = ([(wg_v[c], tg_v[c], 0) for c in range(64)] +
                  [(wu_v[c], tu_v[c], 1) for c in range(64)] +
                  [(wd_v_f(c), td_v_f(c), 2) for c in range(64)])
        wslot = [0]
        wslot_guard = [None, None]
        sgn_guard = [0]
        svec_ready = [0]

        def w_issue(job):
            sl = wslot[0]
            wslot[0] ^= 1
            if wslot_guard[sl] is not None:
                W("sync", wslot_guard[sl][0], wslot_guard[sl][1])
            DMA("sync", xf32b[:, sl, :], job[0], ldw[sl])
            return (sl, job)

        scale_flip = [0]
        acc_guard = [0]

        def w_scale(entry):
            sl, (src, widx) = entry
            scale_flip[0] ^= 1
            if scale_flip[0]:
                W("vector", ldw[sl])
                v_op("tensor_reduce", s_wacc[:, 3:4], xf32b[:, sl, :], AX,
                     ALU.add, apply_absolute_value=True)
                v_op("tensor_add", s_wacc[:, widx:widx + 1],
                     s_wacc[:, widx:widx + 1], s_wacc[:, 3:4])
                wslot_guard[sl] = (chv, chv.n)
            else:
                W("scalar", ldw[sl])
                W("scalar", chv, acc_guard[0])
                a_op("activation", sgn1, xf32b[:, sl, :], ACTF.Abs,
                     accum_out=s_stats[:, 15:16])
                an = cha.n
                W("vector", cha, an)
                v_op("tensor_add", s_wacc[:, widx:widx + 1],
                     s_wacc[:, widx:widx + 1], s_stats[:, 15:16])
                acc_guard[0] = chv.n
                wslot_guard[sl] = (cha, an)

        def w_tern(entry):
            sl, (src, dst, widx) = entry
            W("scalar", ldw[sl])
            W("scalar", tw[sl])
            W("scalar", chv, max(sgn_guard[0], svec_ready[0]))
            a_op("activation", sgn1, xf32b[:, sl, :], ACTF.Sign,
                 bias=s_thvec[:, widx:widx + 1])
            a_op("activation", tstr[sl], xf32b[:, sl, :], ACTF.Sign,
                 bias=s_svec[:, widx:widx + 1])
            s2n = cha.n
            wslot_guard[sl] = (cha, cha.n)
            W("vector", cha, s2n)
            zn = v_op("tensor_add", tstr[sl], tstr[sl], sgn1)
            sgn_guard[0] = zn
            W("gpsimd", chv, zn)
            DMA("gpsimd", dst, tstr[sl], tw[sl])

        def w_finalize():
            W("tensor", chv)
            W("tensor", cha)
            for widx in range(3):
                PE("matmul", pw[0:1, widx:widx + 1], s_onesf[:, 0:1],
                   s_wacc[:, widx:widx + 1], start=True, stop=True,
                   inc=(pe if widx == 2 else None))
            W("vector", pe)
            v_op("tensor_copy", s_wacc[0:1, 0:3], pw[0:1, 0:3])
            sn = chv.n
            W("tensor", chv, sn)
            PE("matmul", pw[:, 4:7], s_onesg[0:1, :], s_wacc[0:1, 0:3],
               start=True, stop=True, inc=pe)
            W("vector", pe)
            v_op("tensor_scalar", s_svec[:, 0:3], pw[:, 4:7],
                 0.5 / float(I8K * D), None, ALU.mult)
            v_op("tensor_scalar", s_thvec[:, 0:3], s_svec[:, 0:3], -1.0, None,
                 ALU.mult)
            svec_ready[0] = chv.n

        pe_slot_guard = [0, 0, 0, 0]
        pbig_act_guard = [0]
        pls_slot = [0, 0]

        def p1_batch(b):
            sl = b % 2
            if b + 1 < NB:
                DMA("sync", xf32a[:, (b + 1) % 2, :],
                    xs[128 * (b + 1):128 * (b + 2), :], ldx[(b + 1) % 2])
            xt = xf32a[:, sl, :]
            xw = 16 * (b // 2 + 1)
            W("vector", ldx[sl], xw)
            W("scalar", ldx[sl], xw)
            v_op("tensor_reduce", mu1[:], xt, AX, ALU.add)
            a_op("activation", sqsb, xt, ACTF.Square, accum_out=sg1[:])
            v_op("tensor_scalar", mu1[:], mu1[:], 1.0 / D, None, ALU.mult)
            W("vector", cha)
            v_op("tensor_scalar", sg1[:], sg1[:], 1.0 / D, None, ALU.mult)
            v_op("tensor_mul", m16[:, 0:1], mu1[:], mu1[:])
            v_op("tensor_sub", sg1[:], sg1[:], m16[:, 0:1])
            vn = chv.n
            W("scalar", chv, vn)
            a_op("activation", sg1[:], sg1[:], ACTF.Sqrt, bias=s_cst[:, 2:3])
            W("vector", cha)
            v_op("reciprocal", sg1[:], sg1[:])
            yn = v_op("tensor_scalar", ya, xt, mu1[:, 0:1], sg1[:, 0:1],
                      ALU.subtract, ALU.mult)

            pwb = pw.bitcast(BF16)
            for g in range(16):
                W("tensor", chv, yn if g == 0 else chv.n)
                PE("transpose", pwb[:, 0:128], ya[:, 128 * g:128 * (g + 1)],
                   s_ident[:], inc=pe)
                W("vector", pe)
                v_op("tensor_copy", xtg, pwb[:, 0:128])
                xn = chv.n
                W("tensor", chv, xn)
                W("tensor", chv, pe_slot_guard[g % 4])
                bd.em("tensor", (lambda v: lambda e: e.wait_ge(cha.h, v))(
                    pbig_act_guard[0]))
                PE("matmul", pbig[:, 512 * (g % 4):512 * (g % 4) + 384], xtg,
                   s_bdqkv[:], start=True, stop=True, inc=pe)
                W("vector", pe)
                v_op("tensor_copy", qkvs[:, 384 * g:384 * (g + 1)],
                     pbig[:, 512 * (g % 4):512 * (g % 4) + 384])
                pe_slot_guard[g % 4] = chv.n

            for off, dst in ((QOFF, b2q), (KOFF, b2k), (VOFF, b2v)):
                for c2 in range(8):
                    src = qkvs[:, off + 8 + c2::48]
                    W("tensor", chv)
                    PE("transpose", pwb[:, 128:256], src, s_ident[:], inc=pe)
                    W("vector", pe)
                    v_op("tensor_copy", dst[:, 128 * c2:128 * (c2 + 1)],
                         pwb[:, 128:256])

            q4 = qkvs.rearrange("p (g t co) -> p g t co", g=16, co=48)

            def softmax_av_write(kind):
                v_op("tensor_reduce",
                     smax.rearrange("p (w x) -> p w x", x=1),
                     sbuf1.rearrange("p (w bb) -> p w bb", bb=8), AX, ALU.max)
                v_op("tensor_scalar", smaxs, smax, SCL8, None, ALU.mult)
                sn = v_op("scalar_tensor_tensor",
                          sbuf1.rearrange("p (w bb) -> p w bb", bb=8),
                          sbuf1.rearrange("p (w bb) -> p w bb", bb=8), SCL8,
                          smaxs.unsqueeze(2).broadcast_to([128, 128, 8]),
                          ALU.mult, ALU.subtract)
                W("scalar", chv, sn)
                en = a_op("activation", sexp, sbuf1, ACTF.Exp)
                W("vector", cha, en)
                v_op("tensor_reduce",
                     ssum.rearrange("p (w x) -> p w x", x=1),
                     sexp.rearrange("p (w bb) -> p w bb", bb=8), AX, ALU.add)
                v_op("reciprocal", srcp, ssum)

            pr5 = prods.rearrange("p (c g a bb) -> p c g a bb", c=8, g=16, a=8)
            qsl = q4[:, :, :, QOFF:QOFF + 8].transpose([0, 3, 1, 2])
            ksl = q4[:, :, :, KOFF:KOFF + 8].transpose([0, 3, 1, 2])
            for cc in range(8):
                v_op("tensor_tensor", pr5[:, cc],
                     qsl[:, cc].unsqueeze(3).broadcast_to([128, 16, 8, 8]),
                     ksl[:, cc].unsqueeze(2).broadcast_to([128, 16, 8, 8]),
                     ALU.mult)
            v_op("tensor_add", prods[:, 0:4096], prods[:, 0:4096],
                 prods[:, 4096:8192])
            v_op("tensor_add", prods[:, 0:2048], prods[:, 0:2048],
                 prods[:, 2048:4096])
            v_op("tensor_add", sbuf1, prods[:, 0:1024], prods[:, 1024:2048])
            softmax_av_write(1)
            pav = prods.rearrange("p (bb g a c) -> p bb g a c", bb=8, g=16, a=8)
            e4 = sexp.rearrange("p (g a bb) -> p g a bb", g=16, a=8)
            vsl = q4[:, :, :, VOFF:VOFF + 8]
            vb = vsl.transpose([0, 2, 1, 3])
            for bb2 in range(8):
                v_op("tensor_tensor", pav[:, bb2],
                     e4[:, :, :, bb2].unsqueeze(3).broadcast_to([128, 16, 8, 8]),
                     vb[:, bb2].unsqueeze(2).broadcast_to([128, 16, 8, 8]),
                     ALU.mult)
            v_op("tensor_add", prods[:, 0:4096], prods[:, 0:4096],
                 prods[:, 4096:8192])
            v_op("tensor_add", prods[:, 0:2048], prods[:, 0:2048],
                 prods[:, 2048:4096])
            v_op("tensor_add", oacc, prods[:, 0:1024], prods[:, 1024:2048])
            ol4 = olep.rearrange("p (g a c) -> p g a c", g=16, a=8)
            tap = s_taps.rearrange("p (br tp c) -> p br tp c", br=2, tp=3)
            pr4 = prods[:, 0:1024].rearrange("p (g a c) -> p g a c", g=16, a=8)
            v_op("tensor_tensor", ol4, vsl,
                 tap[:, 0, 1, :].unsqueeze(1).unsqueeze(2)
                 .broadcast_to([128, 16, 8, 8]), ALU.mult)
            v_op("tensor_tensor", pr4[:, :, 0:7, :], vsl[:, :, 0:7, :],
                 tap[:, 0, 0, :].unsqueeze(1).unsqueeze(2)
                 .broadcast_to([128, 16, 7, 8]), ALU.mult)
            v_op("tensor_add", ol4[:, :, 1:8, :], ol4[:, :, 1:8, :],
                 pr4[:, :, 0:7, :])
            v_op("tensor_tensor", pr4[:, :, 0:7, :], vsl[:, :, 1:8, :],
                 tap[:, 0, 2, :].unsqueeze(1).unsqueeze(2)
                 .broadcast_to([128, 16, 7, 8]), ALU.mult)
            v_op("tensor_add", ol4[:, :, 0:7, :], ol4[:, :, 0:7, :],
                 pr4[:, :, 0:7, :])
            oa4 = oacc.rearrange("p (g a c) -> p g a c", g=16, a=8)
            v_op("tensor_tensor", oa4, oa4,
                 srcp.rearrange("p (g a) -> p g a", g=16).unsqueeze(3)
                 .broadcast_to([128, 16, 8, 8]), ALU.mult)
            v_op("tensor_add", oacc, oacc, olep)
            of1 = ofull.rearrange("p (g a c) -> p g a c", g=16, a=8, c=16)[:, :, :, 0:8]
            v_op("tensor_tensor", of1, oa4,
                 s_cb[:, 0:8].unsqueeze(1).unsqueeze(2)
                 .broadcast_to([128, 16, 8, 8]), ALU.add)

            b2q_v = b2q.rearrange("p (c r) -> p c r", c=8)
            b2k_v = b2k.rearrange("p (c r) -> p c r", c=8)
            b2v_v = b2v.rearrange("p (c r) -> p c r", c=8)
            for oo in range(4):
                pr2 = prods[:, 0:2048].rearrange("p (c a q bb) -> p c a q bb", c=8, a=8, q=4)
                qs = b2q_v[:, :, 32 * oo:32 * oo + 32].rearrange(
                    "p c (a q) -> p c a q", a=8)
                ks = b2k_v[:, :, 32 * oo:32 * oo + 32].rearrange(
                    "p c (b q) -> p c b q", b=8)
                for cc in range(8):
                    v_op("tensor_tensor", pr2[:, cc],
                         qs[:, cc].unsqueeze(3).broadcast_to([128, 8, 4, 8]),
                         ks[:, cc].transpose([0, 2, 1]).unsqueeze(1)
                         .broadcast_to([128, 8, 4, 8]), ALU.mult)
                v_op("tensor_add", prods[:, 0:1024], prods[:, 0:1024],
                     prods[:, 1024:2048])
                v_op("tensor_add", prods[:, 0:512], prods[:, 0:512],
                     prods[:, 512:1024])
                v_op("tensor_add", sbuf1[:, 256 * oo:256 * (oo + 1)],
                     prods[:, 0:256], prods[:, 256:512])
            softmax_av_write(2)
            for oo in range(4):
                pav2 = prods[:, 0:2048].rearrange(
                    "p (bb a q c) -> p bb a q c", bb=8, a=8, q=4)
                es = sexp.rearrange("p (o a q bb) -> p o a q bb", o=4, a=8,
                                    q=4)[:, oo]
                vs = b2v_v[:, :, 32 * oo:32 * oo + 32].rearrange(
                    "p c (b q) -> p c b q", b=8)
                for bb2 in range(8):
                    v_op("tensor_tensor", pav2[:, bb2],
                         es[:, :, :, bb2].unsqueeze(3)
                         .broadcast_to([128, 8, 4, 8]),
                         vs[:, :, bb2, :].transpose([0, 2, 1]).unsqueeze(1)
                         .broadcast_to([128, 8, 4, 8]), ALU.mult)
                v_op("tensor_add", prods[:, 0:1024], prods[:, 0:1024],
                     prods[:, 1024:2048])
                v_op("tensor_add", prods[:, 0:512], prods[:, 0:512],
                     prods[:, 512:1024])
                v_op("tensor_add", oacc[:, 256 * oo:256 * (oo + 1)],
                     prods[:, 0:256], prods[:, 256:512])
            oa2 = oacc.rearrange("p (o a q c) -> p o a q c", o=4, a=8, q=4)
            rcp4 = srcp.rearrange("p (o a q) -> p o a q", o=4, a=8)
            for oo in range(4):
                v_op("tensor_tensor", oa2[:, oo], oa2[:, oo],
                     rcp4[:, oo].unsqueeze(3).broadcast_to([128, 8, 4, 8]),
                     ALU.mult)
            vall = b2v_v.rearrange("p c (o aa q) -> p c o aa q", o=4, aa=8)
            vper = vall.transpose([0, 2, 3, 4, 1])
            ol2 = olep.rearrange("p (o a q c) -> p o a q c", o=4, a=8, q=4)
            pr3 = prods[:, 0:1024].rearrange("p (o a q c) -> p o a q c", o=4, a=8, q=4)
            for oo in range(4):
                v_op("tensor_tensor", ol2[:, oo], vper[:, oo],
                     tap[:, 1, 1, :].unsqueeze(1).unsqueeze(2)
                     .broadcast_to([128, 8, 4, 8]), ALU.mult)
                v_op("tensor_tensor", pr3[:, oo, 0:7], vper[:, oo, 0:7],
                     tap[:, 1, 0, :].unsqueeze(1).unsqueeze(2)
                     .broadcast_to([128, 7, 4, 8]), ALU.mult)
                v_op("tensor_add", ol2[:, oo, 1:8], ol2[:, oo, 1:8],
                     pr3[:, oo, 0:7])
                v_op("tensor_tensor", pr3[:, oo, 0:7], vper[:, oo, 1:8],
                     tap[:, 1, 2, :].unsqueeze(1).unsqueeze(2)
                     .broadcast_to([128, 7, 4, 8]), ALU.mult)
                v_op("tensor_add", ol2[:, oo, 0:7], ol2[:, oo, 0:7],
                     pr3[:, oo, 0:7])
            v_op("tensor_add", oacc, oacc, olep)
            o2w = o2cm.rearrange("p (c rr) -> p c rr", c=8).rearrange(
                "p c (o a q) -> p c o a q", o=4, a=8).transpose([0, 2, 3, 4, 1])
            for oo in range(4):
                b2n = v_op("tensor_tensor", o2w[:, oo], oa2[:, oo],
                           s_cb[:, 8:16].unsqueeze(1).unsqueeze(2)
                           .broadcast_to([128, 8, 4, 8]), ALU.add)
            for c2 in range(8):
                W("tensor", chv, b2n if c2 == 0 else chv.n)
                PE("transpose", pwb[:, 256:384],
                   o2cm[:, 128 * c2:128 * (c2 + 1)], s_ident[:], inc=pe)
                W("vector", pe)
                ofc = ofull.rearrange("p (m c) -> p m c", c=16)[:, :, 8 + c2]
                v_op("tensor_copy", ofc, pwb[:, 256:384])

            for g in range(16):
                W("tensor", chv)
                PE("transpose", pwb[:, 384:512], ofull[:, 128 * g:128 * (g + 1)],
                   s_ident[:], inc=pe)
                W("vector", pe)
                v_op("tensor_copy", xtg, pwb[:, 384:512])
                xn = chv.n
                W("tensor", chv, xn)
                PE("matmul", pbig[:, 128 * g:128 * (g + 1)], xtg, s_bdproj[:],
                   start=True, stop=True, inc=(pe if g == 15 else None))
            W("vector", pe)
            pb16 = pbig.rearrange("p (m c) -> p m c", c=16)
            v_op("tensor_tensor", pb16, pb16,
                 s_pb[:].unsqueeze(1).broadcast_to([128, 128, 16]), ALU.add)
            v_op("tensor_reduce", m16.rearrange("p (m x) -> p m x", x=1), pb16,
                 AX, ALU.add)
            v_op("tensor_scalar", m16, m16, 1.0 / 16, None, ALU.mult)
            mn = chv.n
            W("scalar", chv, mn)
            a_op("activation", sqsb, pbig[:, :], ACTF.Square)
            sqn = cha.n
            W("vector", cha, sqn)
            v_op("tensor_reduce", v16.rearrange("p (m x) -> p m x", x=1),
                 sqsb.rearrange("p (m c) -> p m c", c=16), AX, ALU.add)
            v_op("tensor_scalar", v16, v16, 1.0 / 16, None, ALU.mult)
            v_op("tensor_mul", r16, m16, m16)
            v_op("tensor_sub", v16, v16, r16)
            vn2 = chv.n
            W("scalar", chv, vn2)
            a_op("activation", v16, v16, ACTF.Sqrt, bias=s_cst[:, 2:3])
            W("vector", cha)
            v_op("reciprocal", r16, v16)
            t4 = t16a.rearrange("p (m c) -> p m c", c=16)
            v_op("tensor_tensor", t4, pb16,
                 m16.unsqueeze(2).broadcast_to([128, 128, 16]), ALU.subtract)
            v_op("tensor_tensor", t4, t4,
                 r16.unsqueeze(2).broadcast_to([128, 128, 16]), ALU.mult)
            v_op("tensor_tensor", t4, t4,
                 s_lnw[:].unsqueeze(1).broadcast_to([128, 128, 16]), ALU.mult)
            v_op("tensor_tensor", t4, t4,
                 s_lnb[:].unsqueeze(1).broadcast_to([128, 128, 16]), ALU.add)
            t2w = tok2[sl].rearrange("p (c m) -> p c m", c=16).transpose([0, 2, 1])
            W("vector", pls[sl], pls_slot[sl])
            tn = v_op("tensor_tensor", t2w, t4,
                      ya.rearrange("p (m c) -> p m c", c=16), ALU.add)
            for _i in range(4):
                pe_slot_guard[_i] = chv.n
            pbig_act_guard[0] = cha.n
            W("sync", chv, tn)
            for c in range(16):
                DMA("sync", planes[c, b], tok2[sl][:, 128 * c:128 * (c + 1)],
                    pls[sl])
            pls_slot[sl] = pls[sl].n

        pend_s, pend_t = [], []

        def pump_w(b):
            if b < 6:
                for _ in range(32):
                    if scale_q:
                        pend_s.append(w_issue(scale_q.pop(0)))
                    while len(pend_s) > 1:
                        w_scale(pend_s.pop(0))
                if b == 5:
                    while pend_s:
                        w_scale(pend_s.pop(0))
            elif b == 6:
                w_finalize()
            else:
                n = 22 if b < NB - 1 else 64
                for _ in range(n):
                    if tern_q:
                        pend_t.append(w_issue(tern_q.pop(0)))
                    while len(pend_t) > 1:
                        w_tern(pend_t.pop(0))
                if b == NB - 1:
                    while pend_t:
                        w_tern(pend_t.pop(0))

        for b in range(NB):
            p1_batch(b)
            pump_w(b)

        W("sync", pls[0])
        W("sync", pls[1])
        W("sync", tw[0])
        W("sync", tw[1])
        W("sync", chv)
        W("sync", cha)
        W("sync", chg)
        W("sync", pe)
        bd.em("sync", lambda e: e.sem_inc(bar.h, 1))
        bar.n += 1
        for eng in ("vector", "scalar", "gpsimd", "tensor"):
            W(eng, bar, bar.n)

        if dbg:
            W("gpsimd", chv)
            DMA("gpsimd", sdbg[:, 0:4], s_svec[:], ost[0])
            DMA("gpsimd", sdbg[:, 4:8], s_thvec[:], ost[0])
            DMA("gpsimd", sdbg[:, 8:12], s_wacc[:], ost[0])

        psum8 = [pbig[:, 0:512], pbig[:, 512:1024], pbig[:, 1024:1536],
                 pbig[:, 1536:2048], pq[:], pu[:], ph[:], pw[:]]
        guard8 = [chv.n] * 8
        slot_store = [0, 0, 0, 0]
        dd_guard = [0, 0]
        hm_guard = [0]
        wdt_pe_guard = [0, 0]
        ga_guard = [cha.n, cha.n]
        ost_cnt = [0]
        pwb = pw.bitcast(BF16)

        def quant_chunk(src_is_psum, src_ap, dst_ap, beta_bc, alpha_bc):
            v_op("tensor_tensor", qu[:], src_ap, beta_bc, ALU.mult)
            v_op("tensor_copy", qiq[:], qu[:])
            v_op("tensor_copy", qfq[:], qiq[:])
            dn = v_op("tensor_sub", dqb, qu[:], qfq[:])
            W("scalar", chv, dn)
            a_op("activation", zs1, dqb, ACTF.Sign, bias=s_cst[:, 0:1])
            s2n = a_op("activation", zs2, dqb, ACTF.Sign, bias=s_cst[:, 1:2])
            W("vector", cha, s2n)
            v_op("tensor_add", zzb, zs1, zs2)
            if alpha_bc is None:
                return v_op("scalar_tensor_tensor", dst_ap, zzb, 0.5, qfq[:],
                            ALU.mult, ALU.add)
            v_op("scalar_tensor_tensor", qu[:], zzb, 0.5, qfq[:], ALU.mult,
                 ALU.add)
            return v_op("tensor_tensor", dst_ap, qu[:], alpha_bc, ALU.mult)

        def maxabs_to_rows(nchunks, bufv, ncols, clipval, rows_dst):
            v_op("tensor_reduce", mxf[:].unsqueeze(2),
                 bufv[:, 0:nchunks, :].transpose([0, 2, 1]), AX, ALU.max,
                 apply_absolute_value=True)
            cn = v_op("tensor_copy", mxcast[:, 0:512], mxf[:])
            for ts in range(4):
                W("tensor", chv, cn if ts == 0 else chv.n)
                PE("transpose", pwb[:, 0:128],
                   mxcast[:, 128 * ts:128 * (ts + 1)], s_ident[:], inc=pe)
                W("vector", pe)
                v_op("tensor_reduce", s_mx2[:, ts:ts + 1], pwb[:, 0:128], AX,
                     ALU.max, apply_absolute_value=True)
            v_op("tensor_scalar_max", s_mx2[:], s_mx2[:], clipval)
            if rows_dst is not None:
                v_op("reciprocal", s_a2sd[:], s_mx2[:])
                v_op("tensor_scalar", s_stats[:, 8:12], s_a2sd[:], 127.0,
                     None, ALU.mult)
                cn2 = v_op("tensor_copy", stcast[:, 0:4], s_stats[:, 8:12])
                for ts in range(4):
                    W("tensor", chv, cn2 if ts == 0 else chv.n)
                    PE("transpose", pwb[0:1, 0:128],
                       stcast[:, ts:ts + 1], s_ident[:], inc=pe)
                    W("vector", pe)
                    v_op("tensor_copy", rows_dst[0:1, 128 * ts:128 * (ts + 1)],
                         pwb[0:1, 0:128])
                rn2 = chv.n
                W("tensor", chv, rn2)
                W("tensor", cha)
                PE("matmul", pq[:], s_ones[0:1, :], rows_dst[0:1, 0:512],
                   start=True, stop=True, inc=pe)
                W("vector", pe)
                v_op("tensor_copy", bc2, pq[:])
                return chv.n
            return chv.n

        def p2_supertile(sti):
            base = 4 * sti
            for ci in range(4):
                c = base + ci
                sl = c % 2
                W("sync", pe, wdt_pe_guard[sl])
                W("sync", chv, dd_guard[sl])
                DMA("sync", wdt[sl],
                    planes[c].rearrange("b (J q) m -> (b J) (q m)", J=8),
                    ldp[sl])
                DMA("sync", xf32b[:, sl, :], xres[c], ldr[sl])
                W("vector", ldp[sl])
                W("vector", ldr[sl])
                for _o in ost:
                    W("vector", _o)
                x1n = v_op("tensor_add", xf32a[:, ci % 2, :], xf32b[:, sl, :],
                           wdt[sl])
                dd_guard[sl] = chv.n
                v_op("tensor_reduce", s_stats[:, ci:ci + 1], xf32a[:, ci % 2, :],
                     AX, ALU.add)
                v_op("tensor_scalar", s_stats[:, ci:ci + 1],
                     s_stats[:, ci:ci + 1], 1.0 / D, None, ALU.mult)
                W("scalar", chv, x1n)
                a_op("activation", x1b[:, ci, :], xf32a[:, ci % 2, :],
                     ACTF.Square, accum_out=sg1[:])
                W("vector", cha)
                v_op("tensor_scalar", sg1[:], sg1[:], 1.0 / D, None, ALU.mult)
                v_op("tensor_mul", mu1[:], s_stats[:, ci:ci + 1],
                     s_stats[:, ci:ci + 1])
                v_op("tensor_sub", sg1[:], sg1[:], mu1[:])
                vn = chv.n
                W("scalar", chv, vn)
                a_op("activation", sg1[:], sg1[:], ACTF.Sqrt, bias=s_cst[:, 2:3])
                W("vector", cha)
                v_op("reciprocal", s_stats[:, 4 + ci:5 + ci], sg1[:])
                W("vector", cha)
                bn = v_op("tensor_copy", x1b[:, ci, :], xf32a[:, ci % 2, :])
                for fc in range(16):
                    W("tensor", chv, bn if fc == 0 else chv.n)
                    PE("transpose", pwb[:, 0:128],
                       x1b[:, ci, 128 * fc:128 * (fc + 1)], s_ident[:], inc=pe)
                    W("vector", pe)
                    v_op("tensor_copy", xq1[:, fc, 128 * ci:128 * (ci + 1)],
                         pwb[:, 0:128])
            for fc in range(16):
                W("tensor", chv)
                PE("matmul", ph[:], s_h128[:], xq1[:, fc, :], start=True,
                   stop=True, inc=pe)
                W("vector", pe)
                v_op("tensor_copy", xq1[:, fc, :], ph[:])
            for stg in range(4):
                step = 1 << stg
                for bb in range(0, 16, 2 * step):
                    for j in range(step):
                        Aap = xq1[:, bb + j, :]
                        Bap = xq1[:, bb + j + step, :]
                        v_op("tensor_add", Aap, Aap, Bap)
                        v_op("scalar_tensor_tensor", Bap, Bap, -2.0, Aap,
                             ALU.mult, ALU.add)
            v_op("tensor_scalar", fmisc[:, 0:4], s_stats[:, 0:4], float(D),
                 None, ALU.mult)
            pk0 = v_op("tensor_copy", stcast[:, 0:4], fmisc[:, 0:4])
            for ts in range(4):
                W("tensor", chv, pk0 if ts == 0 else chv.n)
                PE("transpose", pwb[0:1, 0:128],
                   stcast[:, ts:ts + 1], s_ident[:], inc=pe)
                W("vector", pe)
                v_op("tensor_copy",
                     strows[0:1, 128 * ts:128 * (ts + 1)], pwb[0:1, 0:128])
            v_op("tensor_sub", xq1[0:1, 0, :], xq1[0:1, 0, :],
                 strows[0:1, 0:512])
            maxabs_to_rows(16, xq1, 512, EPS1, None)
            v_op("reciprocal", s_a2sd[:], s_mx2[:])
            v_op("tensor_scalar", s_stats[:, 8:12], s_a2sd[:], 127.0, None,
                 ALU.mult)
            v_op("tensor_tensor", s_stats[:, 12:16], s_mx2[:],
                 s_stats[:, 4:8], ALU.mult)
            v_op("tensor_scalar", s_stats[:, 12:16], s_stats[:, 12:16],
                 C2K / 127.0, None, ALU.mult)
            v_op("tensor_copy", stcast[:, 4:8], s_stats[:, 8:12])
            pk = v_op("tensor_copy", stcast[:, 8:12], s_stats[:, 12:16])
            for q3 in range(2):
                for ts in range(4):
                    W("tensor", chv, pk if (q3 == 0 and ts == 0) else chv.n)
                    PE("transpose", pwb[0:1, 0:128],
                       stcast[:, 4 + 4 * q3 + ts:5 + 4 * q3 + ts],
                       s_ident[:], inc=pe)
                    W("vector", pe)
                    v_op("tensor_copy",
                         strows[0:1,
                                512 * (q3 + 1) + 128 * ts:512 * (q3 + 1) + 128 * (ts + 1)],
                         pwb[0:1, 0:128])
            W("tensor", chv)
            W("tensor", cha)
            PE("matmul", pq[:], s_ones[0:1, :], strows[0:1, 512:1024],
               start=True, stop=True, inc=pe)
            PE("matmul", pu[:], s_ones[0:1, :], strows[0:1, 1024:1536],
               start=True, stop=True, inc=pe)
            W("vector", pe)
            v_op("tensor_copy", bcb, pq[:])
            v_op("tensor_copy", bca, pu[:])
            for fc in range(16):
                quant_chunk(False, xq1[:, fc, :], xq1[:, fc, :], bcb, bca)
            xq_done = chv.n

            ldg_need = {}
            for og in range(32):
                sl = og % 2
                if og == 0:
                    for fcc in range(16):
                        DMA("sync", wgu[:, sl, 0, fcc, :],
                            tern_g[256 * og:256 * (og + 1), 128 * fcc:128 * (fcc + 1)],
                            ldg[sl], transpose=True)
                        DMA("scalar", wgu[:, sl, 1, fcc, :],
                            tern_u[256 * og:256 * (og + 1), 128 * fcc:128 * (fcc + 1)],
                            ldg[sl], transpose=True)
                    ldg_need[0] = ldg[sl].n
                if og + 1 < 32:
                    nsl = (og + 1) % 2
                    W("sync", pe)
                    W("scalar", pe)
                    for fcc in range(16):
                        DMA("sync", wgu[:, nsl, 0, fcc, :],
                            tern_g[256 * (og + 1):256 * (og + 2),
                                   128 * fcc:128 * (fcc + 1)],
                            ldg[nsl], transpose=True)
                        DMA("scalar", wgu[:, nsl, 1, fcc, :],
                            tern_u[256 * (og + 1):256 * (og + 2),
                                   128 * fcc:128 * (fcc + 1)],
                            ldg[nsl], transpose=True)
                    ldg_need[og + 1] = ldg[nsl].n
                bd.em("tensor", (lambda s, v: (lambda e: e.wait_ge(s.h, v)))(
                    ldg[sl], ldg_need[og]))
                if og == 0:
                    bd.em("tensor", (lambda v: lambda e: e.wait_ge(chv.h, v))(
                        xq_done))
                for oc in range(2):
                    o = 2 * og + oc
                    pg_i, pu_i = 4, 5
                    bd.em("tensor", (lambda v: lambda e: e.wait_ge(cha.h, v))(
                        ga_guard[0]))
                    for fcc in range(16):
                        PE("matmul", psum8[pg_i], wgu[:, sl, 0, fcc,
                                                      128 * oc:128 * (oc + 1)],
                           xq1[:, fcc, :], start=(fcc == 0), stop=(fcc == 15),
                           inc=(pe if fcc == 15 else None))
                    bd.em("tensor", (lambda v: lambda e: e.wait_ge(cha.h, v))(
                        ga_guard[1]))
                    for fcc in range(16):
                        PE("matmul", psum8[pu_i], wgu[:, sl, 1, fcc,
                                                      128 * oc:128 * (oc + 1)],
                           xq1[:, fcc, :], start=(fcc == 0), stop=(fcc == 15),
                           inc=(pe if fcc == 15 else None))
                    W("scalar", pe)
                    W("scalar", chv, hm_guard[0])
                    a_op("activation", gsb, psum8[pg_i], ACTF.Silu,
                         scale=s_svec[:, 0:1])
                    ga_guard[0] = cha.n
                    a_op("activation", usb, psum8[pu_i], ACTF.Copy,
                         scale=s_svec[:, 1:2])
                    ga_guard[1] = cha.n
                    W("vector", cha)
                    v_op("tensor_mul", hbuf[:, o, :], gsb, usb)
                    hm_guard[0] = chv.n

            for stg in range(6):
                step = 1 << stg
                for bb in range(0, 64, 2 * step):
                    for j in range(step):
                        Aap = hbuf[:, bb + j, :]
                        Bap = hbuf[:, bb + j + step, :]
                        v_op("tensor_add", Aap, Aap, Bap)
                        v_op("scalar_tensor_tensor", Bap, Bap, -2.0, Aap,
                             ALU.mult, ALU.add)
            maxabs_to_rows(64, hbuf, 512, EPS2, strows)
            v_op("tensor_scalar", s_a2sd[:], s_mx2[:], C8K / 127.0, None,
                 ALU.mult)
            v_op("tensor_scalar", s_a2sd[:], s_a2sd[:], s_svec[:, 2:3], None,
                 ALU.mult)
            for o in range(64):
                W("tensor", chv)
                PE("matmul", ph[:], s_h128[:], hbuf[:, o, :], start=True,
                   stop=True, inc=pe)
                W("vector", pe)
                quant_chunk(True, ph[:], hbuf[:, o, :], bc2, None)
            q2_done = chv.n

            for foh in range(2):
                for ic in range(64):
                    sl = ic % 2
                    W("scalar", pe, wdt_pe_guard[sl])
                    W("scalar", chv, dd_guard[sl])
                    DMA("scalar", wdt[sl][0:128, 0:1024],
                        tern_d[1024 * foh:1024 * (foh + 1),
                               128 * ic:128 * (ic + 1)], ldd[sl],
                        transpose=True)
                    bd.em("tensor", (lambda s, v: lambda e: e.wait_ge(s.h, v))(
                        ldd[sl], ldd[sl].n))
                    if ic == 0:
                        bd.em("tensor", (lambda v: lambda e: e.wait_ge(chv.h, v))(
                            q2_done))
                        for bk in range(8):
                            bd.em("tensor",
                                  (lambda v: lambda e: e.wait_ge(chv.h, v))(
                                      guard8[bk]))
                    for ts in range(4):
                        for fo2 in range(2):
                            bk = ts * 2 + fo2
                            PE("matmul", psum8[bk],
                               hbuf[:, ic, 128 * ts:128 * (ts + 1)],
                               wdt[sl][:, 512 * fo2:512 * (fo2 + 1)],
                               start=(ic == 0), stop=(ic == 63),
                               inc=(pe if bk == 7 else None))
                    wdt_pe_guard[sl] = pe.n
                W("vector", pe)
                for ts in range(4):
                    c = base + ts
                    for fo2 in range(2):
                        bk = ts * 2 + fo2
                        col = 1024 * foh + 512 * fo2
                        stg_sl = (ts * 2 + fo2) % 4
                        stg_ap = xf32a.rearrange("p s n -> p (s n)")[
                            :, 512 * stg_sl:512 * (stg_sl + 1)]
                        W("vector", ost[stg_sl])
                        v_op("scalar_tensor_tensor", stg_ap, psum8[bk],
                             s_a2sd[:, ts:ts + 1], x1b[:, ts, col:col + 512],
                             ALU.mult, ALU.add)
                        guard8[bk] = chv.n
                        W("gpsimd", chv, chv.n)
                        DMA("gpsimd", out[c, :, col:col + 512], stg_ap,
                            ost[stg_sl])
                        ost_cnt[0] += 1

        for sti in range(4):
            p2_supertile(sti)

        for _o in ost:
            W("gpsimd", _o)
        if dbg:
            W("gpsimd", pls[0])
            W("gpsimd", pls[1])

        @block.sync
        def _(e):
            for f in bd.prog["sync"]:
                f(e)

        @block.vector
        def _(e):
            for f in bd.prog["vector"]:
                f(e)

        @block.scalar
        def _(e):
            for f in bd.prog["scalar"]:
                f(e)

        @block.gpsimd
        def _(e):
            for f in bd.prog["gpsimd"]:
                f(e)

        @block.tensor
        def _(e):
            for f in bd.prog["tensor"]:
                f(e)

    return nc


def _host_constants(qkv_w, proj_w, cw0, cb0, cw1, cb1, ln_w, ln_b, proj_b):
    bd_qkv = np.zeros((128, 384), np.float32)
    for t in range(8):
        bd_qkv[t * 16:(t + 1) * 16, t * 48:(t + 1) * 48] = qkv_w
    bd_proj = np.zeros((128, 128), np.float32)
    for t in range(8):
        bd_proj[t * 16:(t + 1) * 16, t * 16:(t + 1) * 16] = proj_w
    h128 = np.ascontiguousarray(_hadamard128().astype(ml_dtypes.bfloat16))
    ident = np.ascontiguousarray(np.eye(128, dtype=ml_dtypes.bfloat16))
    taps = np.zeros((1, 48), np.float32)
    for c in range(8):
        for tp in range(3):
            taps[0, 0 + tp * 8 + c] = cw0[c, 0, 1, tp]
            taps[0, 24 + tp * 8 + c] = cw1[c, 0, tp, 1]
    taps = np.ascontiguousarray(np.repeat(taps, 128, axis=0))
    cb = np.ascontiguousarray(
        np.concatenate([cb0, cb1]).reshape(1, 16).repeat(128, axis=0)
        .astype(np.float32))
    lnw = np.ascontiguousarray(
        np.asarray(ln_w, np.float32).reshape(1, 16).repeat(128, axis=0))
    lnb = np.ascontiguousarray(
        np.asarray(ln_b, np.float32).reshape(1, 16).repeat(128, axis=0))
    pb = np.ascontiguousarray(
        np.asarray(proj_b, np.float32).reshape(1, 16).repeat(128, axis=0))
    return dict(c_bdqkv=bd_qkv, c_bdproj=bd_proj, c_h128=h128, c_ident=ident,
                c_taps=taps, c_cb=cb, c_lnw=lnw, c_lnb=lnb, c_pb=pb)


def kernel(L0_attn_x, L0_attn_qkv_w, L0_attn_conv_w0, L0_attn_conv_b0,
           L0_attn_conv_w1, L0_attn_conv_b1, L0_attn_proj_w, L0_attn_proj_b,
           L0_attn_ln_w, L0_attn_ln_b, L0_ffn_w_gate, L0_ffn_w_up,
           L0_ffn_w_down):
    x = np.ascontiguousarray(np.asarray(L0_attn_x, np.float32)[0])
    consts = _host_constants(np.asarray(L0_attn_qkv_w, np.float32),
                             np.asarray(L0_attn_proj_w, np.float32),
                             np.asarray(L0_attn_conv_w0, np.float32),
                             np.asarray(L0_attn_conv_b0, np.float32),
                             np.asarray(L0_attn_conv_w1, np.float32),
                             np.asarray(L0_attn_conv_b1, np.float32),
                             np.asarray(L0_attn_ln_w, np.float32),
                             np.asarray(L0_attn_ln_b, np.float32),
                             np.asarray(L0_attn_proj_b, np.float32))
    wg = np.ascontiguousarray(np.asarray(L0_ffn_w_gate, np.float32))
    wu = np.ascontiguousarray(np.asarray(L0_ffn_w_up, np.float32))
    wd = np.ascontiguousarray(np.asarray(L0_ffn_w_down, np.float32))

    nc = build_nc()
    in_maps = []
    for k in range(8):
        xres_k = np.ascontiguousarray(
            np.stack([x[c * 1024 + 128 * k: c * 1024 + 128 * k + 128]
                      for c in range(NT)]))
        m = {"xs": np.ascontiguousarray(x[2048 * k:2048 * (k + 1)]),
             "xres": xres_k, "w_g": wg, "w_u": wu, "w_d": wd}
        m.update(consts)
        in_maps.append(m)
    res = run_bass_kernel_spmd(nc, in_maps, core_ids=list(range(8)))
    outp = np.empty((16384, 2048), np.float32)
    for k in range(8):
        ok = res.results[k]["out"]
        for c in range(NT):
            outp[c * 1024 + 128 * k: c * 1024 + 128 * k + 128] = ok[c]
    return outp.reshape(1, 16384, 2048)


# version 07
# speedup vs baseline: 1.2720x; 1.2720x over previous
import numpy as np
import ml_dtypes
import concourse.bass as bass
import concourse.mybir as mybir
from concourse.bass_utils import run_bass_kernel_spmd

F32 = mybir.dt.float32
BF16 = mybir.dt.bfloat16
I32 = mybir.dt.int32
ALU = mybir.AluOpType
ACTF = mybir.ActivationFunctionType
AX = mybir.AxisListType.X

R = 2048
D = 2048
NB = 16
NT = 16
I8K = 8192
EPS = 1e-5
C2K = float(1.0 / np.sqrt(np.float64(2048.0)))
C8K = float(1.0 / np.sqrt(np.float64(8192.0)))
SCL8 = float(1.0 / np.sqrt(np.float64(8.0)))
EPS1 = float(EPS / C2K)
EPS2 = float(EPS / C8K)

QOFF, KOFF, VOFF = 0, 16, 32


def _hadamard128():
    h = np.ones((1, 1), np.float32)
    while h.shape[0] < 128:
        h = np.block([[h, h], [h, -h]])
    return h


class Sem:
    def __init__(self, h):
        self.h = h
        self.n = 0


class Builder:

    def __init__(self, nc):
        self.nc = nc
        self.prog = {k: [] for k in ("sync", "vector", "scalar", "gpsimd", "tensor")}

    def em(self, eng, fn):
        self.prog[eng].append(fn)

    def chain_op(self, eng, chain, fn_name, *a, **kw):
        n0 = chain.n
        import traceback
        site = traceback.extract_stack()[-3]

        def f(e):
            if n0:
                e.wait_ge(chain.h, n0)
            try:
                getattr(e, fn_name)(*a, **kw).then_inc(chain.h, 1)
            except Exception as ex:
                raise RuntimeError(
                    f"{fn_name} from {site.filename}:{site.lineno}: {ex}"
                ) from ex

        self.em(eng, f)
        chain.n += 1
        return chain.n

    def wait(self, eng, sem, val=None):
        v = sem.n if val is None else val
        if v > 0:
            self.em(eng, lambda e: e.wait_ge(sem.h, v))

    def pe(self, fn_name, *a, inc=None, **kw):

        def f(e):
            inst = getattr(e, fn_name)(*a, **kw)
            if inc is not None:
                inst.then_inc(inc.h, 1)

        self.em("tensor", f)
        if inc is not None:
            inc.n += 1
            return inc.n

    def dma(self, eng, out_ap, in_ap, sem, transpose=False):
        def f(e):
            if transpose:
                inst = e.dma_start_transpose(out_ap, in_ap)
            else:
                inst = e.dma_start(out=out_ap, in_=in_ap)
            inst.then_inc(sem.h, 16)

        self.em(eng, f)
        sem.n += 16
        return sem.n


def build_nc(dbg=False):
    nc = bass.Bass("TRN2", target_bir_lowering=False, debug=False)

    xs = nc.dram_tensor("xs", [R, D], F32, kind="ExternalInput").ap()
    xres = nc.dram_tensor("xres", [NT, 128, D], F32, kind="ExternalInput").ap()
    w_g = nc.dram_tensor("w_g", [I8K, D], F32, kind="ExternalInput").ap()
    w_u = nc.dram_tensor("w_u", [I8K, D], F32, kind="ExternalInput").ap()
    w_d = nc.dram_tensor("w_d", [D, I8K], F32, kind="ExternalInput").ap()
    c_bdqkv = nc.dram_tensor("c_bdqkv", [128, 384], F32, kind="ExternalInput").ap()
    c_bdproj = nc.dram_tensor("c_bdproj", [128, 128], F32, kind="ExternalInput").ap()
    c_h128 = nc.dram_tensor("c_h128", [128, 128], BF16, kind="ExternalInput").ap()
    c_ident = nc.dram_tensor("c_ident", [128, 128], BF16, kind="ExternalInput").ap()
    c_taps = nc.dram_tensor("c_taps", [128, 48], F32, kind="ExternalInput").ap()
    c_cb = nc.dram_tensor("c_cb", [128, 16], F32, kind="ExternalInput").ap()
    c_lnw = nc.dram_tensor("c_lnw", [128, 16], F32, kind="ExternalInput").ap()
    c_lnb = nc.dram_tensor("c_lnb", [128, 16], F32, kind="ExternalInput").ap()
    c_pb = nc.dram_tensor("c_pb", [128, 16], F32, kind="ExternalInput").ap()

    kind_i = "ExternalOutput" if dbg else "Internal"
    tern_g = nc.dram_tensor("tern_g", [I8K, D], BF16, kind=kind_i).ap()
    tern_u = nc.dram_tensor("tern_u", [I8K, D], BF16, kind="Internal").ap()
    tern_d = nc.dram_tensor("tern_d", [D, I8K], BF16, kind="Internal").ap()
    ternT_g = nc.dram_tensor("ternT_g", [64, 16, 128, 128], BF16, kind="Internal").ap()
    ternT_u = nc.dram_tensor("ternT_u", [64, 16, 128, 128], BF16, kind="Internal").ap()
    ternT_d = nc.dram_tensor("ternT_d", [I8K, D], BF16, kind="Internal").ap()
    planes = nc.dram_tensor("planes", [NT, NB, 128, 128], BF16, kind=kind_i).ap()
    sdbg = nc.dram_tensor("sdbg", [128, 12], F32, kind=kind_i).ap() if dbg else None
    out = nc.dram_tensor("out", [NT, 128, D], F32, kind="ExternalOutput").ap()

    import contextlib
    with contextlib.ExitStack() as _stack:
        _E = _stack.enter_context
        s_bdqkv = _E(nc.sbuf_tensor("s_bdqkv", [128, 384], BF16))
        s_bdproj = _E(nc.sbuf_tensor("s_bdproj", [128, 128], BF16))
        s_h128 = _E(nc.sbuf_tensor("s_h128", [128, 128], BF16))
        s_ident = _E(nc.sbuf_tensor("s_ident", [128, 128], BF16))
        s_taps = _E(nc.sbuf_tensor("s_taps", [128, 48], F32))
        s_cb = _E(nc.sbuf_tensor("s_cb", [128, 16], F32))
        s_lnw = _E(nc.sbuf_tensor("s_lnw", [128, 16], F32))
        s_lnb = _E(nc.sbuf_tensor("s_lnb", [128, 16], F32))
        s_pb = _E(nc.sbuf_tensor("s_pb", [128, 16], F32))
        s_cst = _E(nc.sbuf_tensor("s_cst", [128, 4], F32))
        s_ones = _E(nc.sbuf_tensor("s_ones", [128, 128], BF16))
        s_onesf = _E(nc.sbuf_tensor("s_onesf", [128, 1], F32))
        s_onesg = _E(nc.sbuf_tensor("s_onesg", [128, 128], F32))
        s_svec = _E(nc.sbuf_tensor("s_svec", [128, 4], F32))
        s_thvec = _E(nc.sbuf_tensor("s_thvec", [128, 4], F32))
        s_wacc = _E(nc.sbuf_tensor("s_wacc", [128, 4], F32))
        s_stats = _E(nc.sbuf_tensor("s_stats", [128, 16], F32))
        s_mx2 = _E(nc.sbuf_tensor("s_mx2", [128, 4], F32))
        s_a2sd = _E(nc.sbuf_tensor("s_a2sd", [128, 4], F32))
        s_rows = _E(nc.sbuf_tensor("s_rows", [128, 2592], BF16))
        xf32a = _E(nc.sbuf_tensor("xf32a", [128, 2, 2048], F32))
        xf32b = _E(nc.sbuf_tensor("xf32b", [128, 2, 2048], F32))
        fmisc = _E(nc.sbuf_tensor("fmisc", [128, 160], F32))
        f128 = _E(nc.sbuf_tensor("f128", [128, 768], F32))
        mu1 = _E(nc.sbuf_tensor("mu1", [128, 1], F32))
        sg1 = _E(nc.sbuf_tensor("sg1", [128, 1], F32))
        qu = _E(nc.sbuf_tensor("qu", [128, 512], F32))
        qiq = _E(nc.sbuf_tensor("qiq", [128, 512], I32))
        qfq = _E(nc.sbuf_tensor("qfq", [128, 512], F32))
        mxf = _E(nc.sbuf_tensor("mxf", [128, 512], F32))
        A = _E(nc.sbuf_tensor("A", [128, 32768], BF16))
        B = _E(nc.sbuf_tensor("B", [128, 16384], BF16))
        C = _E(nc.sbuf_tensor("C", [128, 8192], BF16))
        DD = _E(nc.sbuf_tensor("DD", [128, 4096], BF16))
        x1b = _E(nc.sbuf_tensor("x1b", [128, 4, 2048], BF16))
        sm = _E(nc.sbuf_tensor("sm", [128, 4096], BF16))
        pbig = _E(nc.psum_tensor("pbig", [128, 2048], F32))
        pq = _E(nc.psum_tensor("pq", [128, 512], F32))
        pu = _E(nc.psum_tensor("pu", [128, 512], F32))
        ph = _E(nc.psum_tensor("ph", [128, 512], F32))
        pw = _E(nc.psum_tensor("pw", [128, 512], F32))
        chv_h = _E(nc.semaphore("chv"))
        cha_h = _E(nc.semaphore("cha"))
        chg_h = _E(nc.semaphore("chg"))
        pe_h = _E(nc.semaphore("pe0"))
        ldc_h = _E(nc.semaphore("ldc"))
        ldx0_h = _E(nc.semaphore("ldx0"))
        ldx1_h = _E(nc.semaphore("ldx1"))
        ldw0_h = _E(nc.semaphore("ldw0"))
        ldw1_h = _E(nc.semaphore("ldw1"))
        tw0_h = _E(nc.semaphore("tw0"))
        tw1_h = _E(nc.semaphore("tw1"))
        pls_h0 = _E(nc.semaphore("pls0"))
        pls_h1 = _E(nc.semaphore("pls1"))
        ldg0_h = _E(nc.semaphore("ldg0"))
        ldg1_h = _E(nc.semaphore("ldg1"))
        ldd0_h = _E(nc.semaphore("ldd0"))
        ldd1_h = _E(nc.semaphore("ldd1"))
        ldp0_h = _E(nc.semaphore("ldp0"))
        ldp1_h = _E(nc.semaphore("ldp1"))
        ldr0_h = _E(nc.semaphore("ldr0"))
        ldr1_h = _E(nc.semaphore("ldr1"))
        ost_h0 = _E(nc.semaphore("ost0"))
        ost_h1 = _E(nc.semaphore("ost1"))
        ost_h2 = _E(nc.semaphore("ost2"))
        ost_h3 = _E(nc.semaphore("ost3"))
        bar_h = _E(nc.semaphore("bar"))
        block = _E(nc.Block())
        bd = Builder(nc)
        chv, cha, chg, pe = Sem(chv_h), Sem(cha_h), Sem(chg_h), Sem(pe_h)
        ldc = Sem(ldc_h)
        ldx = [Sem(ldx0_h), Sem(ldx1_h)]
        ldw = [Sem(ldw0_h), Sem(ldw1_h)]
        tw = [Sem(tw0_h), Sem(tw1_h)]
        pls = [Sem(pls_h0), Sem(pls_h1)]
        ldg = [Sem(ldg0_h), Sem(ldg1_h)]
        ldd = [Sem(ldd0_h), Sem(ldd1_h)]
        ldp = [Sem(ldp0_h), Sem(ldp1_h)]
        ldr = [Sem(ldr0_h), Sem(ldr1_h)]
        ost = [Sem(ost_h0), Sem(ost_h1), Sem(ost_h2), Sem(ost_h3)]
        bar = Sem(bar_h)

        def v_op(fn, *a, **k):
            return bd.chain_op("vector", chv, fn, *a, **k)

        def a_op(fn, *a, **k):
            return bd.chain_op("scalar", cha, fn, *a, **k)

        def g_op(fn, *a, **k):
            return bd.chain_op("gpsimd", chg, fn, *a, **k)

        W = bd.wait
        PE = bd.pe
        DMA = bd.dma

        def a_slice(start, n):
            return A[:, start:start + n]

        prods = a_slice(0, 8192)
        qkvs = a_slice(8192, 6144)
        sbuf1 = a_slice(14336, 1024)
        sexp = a_slice(15360, 1024)
        oacc = a_slice(16384, 1024)
        olep = a_slice(17408, 1024)
        o2cm = a_slice(18432, 1024)
        ofull = a_slice(19456, 2048)
        t16a = a_slice(21504, 2048)
        b2q = a_slice(23552, 1024)
        b2k = a_slice(24576, 1024)
        b2v = a_slice(25600, 1024)
        ya = a_slice(26624, 2048)
        sqsb = a_slice(28672, 2048)
        tok2 = [B[:, 0:2048], B[:, 2048:4096]]
        tstr = [DD[:, 0:2048], DD[:, 2048:4096]]
        sgn1 = x1b[:, 3, :]
        xtg = C[:, 0:128]
        hbuf = A.rearrange("p (c n) -> p c n", n=512)
        xq1 = C.rearrange("p (c n) -> p c n", n=512)
        wgu = B.rearrange("p (s gu f o) -> p s gu f o", s=2, gu=2, f=16)
        wdt = [DD[:, 0:2048], DD[:, 2048:4096]]
        smax = f128[:, 0:128]
        smaxs = f128[:, 128:256]
        ssum = f128[:, 256:384]
        srcp = f128[:, 384:512]
        m16 = f128[:, 512:640]
        v16 = f128[:, 640:768]
        r16 = fmisc[:, 0:128]
        gsb = sm[:, 0:512]
        usb = sm[:, 512:1024]
        dqb = sm[:, 1024:1536]
        zs1 = sm[:, 1536:2048]
        zs2 = sm[:, 2048:2560]
        zzb = sm[:, 2560:3072]
        bca = sm[:, 3072:3584]
        bcb = sm[:, 3584:4096]
        bc2 = s_rows[:, 1536:2048]
        strows = s_rows[:, 0:1536]
        mxcast = s_rows[:, 2048:2560]
        stcast = s_rows[:, 2560:2592]

        DMA("sync", s_h128[:], c_h128, ldc)
        DMA("sync", s_ident[:], c_ident, ldc)
        DMA("sync", s_taps[:], c_taps, ldc)
        DMA("sync", s_cb[:], c_cb, ldc)
        DMA("sync", s_lnw[:], c_lnw, ldc)
        DMA("sync", s_lnb[:], c_lnb, ldc)
        DMA("sync", s_pb[:], c_pb, ldc)
        DMA("sync", xf32a[:, 0, 0:384], c_bdqkv, ldc)
        DMA("sync", xf32a[:, 0, 384:512], c_bdproj, ldc)
        W("vector", ldc)
        v_op("tensor_copy", s_bdqkv[:], xf32a[:, 0, 0:384])
        v_op("tensor_copy", s_bdproj[:], xf32a[:, 0, 384:512])
        v_op("memset", s_cst[:, 0:1], -0.5)
        v_op("memset", s_cst[:, 1:2], 0.5)
        v_op("memset", s_cst[:, 2:3], EPS)
        v_op("memset", s_ones[:], 1.0)
        v_op("memset", s_onesf[:], 1.0)
        v_op("memset", s_onesg[:], 1.0)
        v_op("memset", s_wacc[:], 0.0)
        consts_v = chv.n

        W("sync", chv, consts_v)
        DMA("sync", xf32a[:, 0, :], xs[0:128, :], ldx[0])

        wg_v = w_g.rearrange("(c p) d -> c p d", p=128)
        wu_v = w_u.rearrange("(c p) d -> c p d", p=128)
        def wd_v_f(ce):
            c0, e0 = ce // 4, ce % 4
            return w_d[128 * c0:128 * (c0 + 1), 2048 * e0:2048 * (e0 + 1)]
        tg_v = tern_g.rearrange("(c p) d -> c p d", p=128)
        tu_v = tern_u.rearrange("(c p) d -> c p d", p=128)
        def td_v_f(ce):
            c0, e0 = ce // 4, ce % 4
            return tern_d[128 * c0:128 * (c0 + 1), 2048 * e0:2048 * (e0 + 1)]

        scale_q = ([(wg_v[c], 0) for c in range(64)] +
                   [(wu_v[c], 1) for c in range(64)] +
                   [(wd_v_f(c), 2) for c in range(64)])
        tern_q = ([(wg_v[c], tg_v[c], 0) for c in range(64)] +
                  [(wu_v[c], tu_v[c], 1) for c in range(64)] +
                  [(wd_v_f(c), td_v_f(c), 2) for c in range(64)])
        wslot = [0]
        wslot_guard = [None, None]
        sgn_guard = [0]
        svec_ready = [0]
        tern_left = [64, 64, 64]
        tern_done = {}
        tp_jobs = [(widx, t) for widx in range(3) for t in range(64)]

        def w_issue(job):
            sl = wslot[0]
            wslot[0] ^= 1
            if wslot_guard[sl] is not None:
                W("sync", wslot_guard[sl][0], wslot_guard[sl][1])
            DMA("sync", xf32b[:, sl, :], job[0], ldw[sl])
            return (sl, job)

        scale_flip = [0]
        acc_guard = [0]

        def w_scale(entry):
            sl, (src, widx) = entry
            scale_flip[0] ^= 1
            if scale_flip[0]:
                W("vector", ldw[sl])
                v_op("tensor_reduce", s_wacc[:, 3:4], xf32b[:, sl, :], AX,
                     ALU.add, apply_absolute_value=True)
                v_op("tensor_add", s_wacc[:, widx:widx + 1],
                     s_wacc[:, widx:widx + 1], s_wacc[:, 3:4])
                wslot_guard[sl] = (chv, chv.n)
            else:
                W("scalar", ldw[sl])
                W("scalar", chv, acc_guard[0])
                a_op("activation", sgn1, xf32b[:, sl, :], ACTF.Abs,
                     accum_out=s_stats[:, 15:16])
                an = cha.n
                W("vector", cha, an)
                v_op("tensor_add", s_wacc[:, widx:widx + 1],
                     s_wacc[:, widx:widx + 1], s_stats[:, 15:16])
                acc_guard[0] = chv.n
                wslot_guard[sl] = (cha, an)

        def w_tern(entry):
            sl, (src, dst, widx) = entry
            W("scalar", ldw[sl])
            W("scalar", tw[sl])
            W("scalar", chv, max(sgn_guard[0], svec_ready[0]))
            a_op("activation", sgn1, xf32b[:, sl, :], ACTF.Sign,
                 bias=s_thvec[:, widx:widx + 1])
            a_op("activation", tstr[sl], xf32b[:, sl, :], ACTF.Sign,
                 bias=s_svec[:, widx:widx + 1])
            s2n = cha.n
            wslot_guard[sl] = (cha, cha.n)
            W("vector", cha, s2n)
            zn = v_op("tensor_add", tstr[sl], tstr[sl], sgn1)
            sgn_guard[0] = zn
            W("gpsimd", chv, zn)
            DMA("gpsimd", dst, tstr[sl], tw[sl])
            tern_left[widx] -= 1
            if tern_left[widx] == 0:
                tern_done[widx] = (tw[0].n, tw[1].n)

        def w_finalize():
            W("tensor", chv)
            W("tensor", cha)
            for widx in range(3):
                PE("matmul", pw[0:1, widx:widx + 1], s_onesf[:, 0:1],
                   s_wacc[:, widx:widx + 1], start=True, stop=True,
                   inc=(pe if widx == 2 else None))
            W("vector", pe)
            v_op("tensor_copy", s_wacc[0:1, 0:3], pw[0:1, 0:3])
            sn = chv.n
            W("tensor", chv, sn)
            PE("matmul", pw[:, 4:7], s_onesg[0:1, :], s_wacc[0:1, 0:3],
               start=True, stop=True, inc=pe)
            W("vector", pe)
            v_op("tensor_scalar", s_svec[:, 0:3], pw[:, 4:7],
                 0.5 / float(I8K * D), None, ALU.mult)
            v_op("tensor_scalar", s_thvec[:, 0:3], s_svec[:, 0:3], -1.0, None,
                 ALU.mult)
            svec_ready[0] = chv.n

        pe_slot_guard = [0, 0, 0, 0]
        pbig_act_guard = [0]
        pls_slot = [0, 0]

        def p1_batch(b):
            sl = b % 2
            if b + 1 < NB:
                DMA("sync", xf32a[:, (b + 1) % 2, :],
                    xs[128 * (b + 1):128 * (b + 2), :], ldx[(b + 1) % 2])
            xt = xf32a[:, sl, :]
            xw = 16 * (b // 2 + 1)
            W("vector", ldx[sl], xw)
            W("scalar", ldx[sl], xw)
            v_op("tensor_reduce", mu1[:], xt, AX, ALU.add)
            a_op("activation", sqsb, xt, ACTF.Square, accum_out=sg1[:])
            v_op("tensor_scalar", mu1[:], mu1[:], 1.0 / D, None, ALU.mult)
            W("vector", cha)
            v_op("tensor_scalar", sg1[:], sg1[:], 1.0 / D, None, ALU.mult)
            v_op("tensor_mul", m16[:, 0:1], mu1[:], mu1[:])
            v_op("tensor_sub", sg1[:], sg1[:], m16[:, 0:1])
            vn = chv.n
            W("scalar", chv, vn)
            a_op("activation", sg1[:], sg1[:], ACTF.Sqrt, bias=s_cst[:, 2:3])
            W("vector", cha)
            v_op("reciprocal", sg1[:], sg1[:])
            yn = v_op("tensor_scalar", ya, xt, mu1[:, 0:1], sg1[:, 0:1],
                      ALU.subtract, ALU.mult)

            pwb = pw.bitcast(BF16)
            for g in range(16):
                W("tensor", chv, yn if g == 0 else chv.n)
                PE("transpose", pwb[:, 0:128], ya[:, 128 * g:128 * (g + 1)],
                   s_ident[:], inc=pe)
                W("vector", pe)
                v_op("tensor_copy", xtg, pwb[:, 0:128])
                xn = chv.n
                W("tensor", chv, xn)
                W("tensor", chv, pe_slot_guard[g % 4])
                bd.em("tensor", (lambda v: lambda e: e.wait_ge(cha.h, v))(
                    pbig_act_guard[0]))
                PE("matmul", pbig[:, 512 * (g % 4):512 * (g % 4) + 384], xtg,
                   s_bdqkv[:], start=True, stop=True, inc=pe)
                W("vector", pe)
                v_op("tensor_copy", qkvs[:, 384 * g:384 * (g + 1)],
                     pbig[:, 512 * (g % 4):512 * (g % 4) + 384])
                pe_slot_guard[g % 4] = chv.n

            for off, dst in ((QOFF, b2q), (KOFF, b2k), (VOFF, b2v)):
                for c2 in range(8):
                    src = qkvs[:, off + 8 + c2::48]
                    W("tensor", chv)
                    PE("transpose", pwb[:, 128:256], src, s_ident[:], inc=pe)
                    W("vector", pe)
                    v_op("tensor_copy", dst[:, 128 * c2:128 * (c2 + 1)],
                         pwb[:, 128:256])

            q4 = qkvs.rearrange("p (g t co) -> p g t co", g=16, co=48)

            def softmax_av_write(kind):
                v_op("tensor_reduce",
                     smax.rearrange("p (w x) -> p w x", x=1),
                     sbuf1.rearrange("p (w bb) -> p w bb", bb=8), AX, ALU.max)
                v_op("tensor_scalar", smaxs, smax, SCL8, None, ALU.mult)
                sn = v_op("scalar_tensor_tensor",
                          sbuf1.rearrange("p (w bb) -> p w bb", bb=8),
                          sbuf1.rearrange("p (w bb) -> p w bb", bb=8), SCL8,
                          smaxs.unsqueeze(2).broadcast_to([128, 128, 8]),
                          ALU.mult, ALU.subtract)
                W("scalar", chv, sn)
                en = a_op("activation", sexp, sbuf1, ACTF.Exp)
                W("vector", cha, en)
                v_op("tensor_reduce",
                     ssum.rearrange("p (w x) -> p w x", x=1),
                     sexp.rearrange("p (w bb) -> p w bb", bb=8), AX, ALU.add)
                v_op("reciprocal", srcp, ssum)

            pr5 = prods.rearrange("p (c g a bb) -> p c g a bb", c=8, g=16, a=8)
            qsl = q4[:, :, :, QOFF:QOFF + 8].transpose([0, 3, 1, 2])
            ksl = q4[:, :, :, KOFF:KOFF + 8].transpose([0, 3, 1, 2])
            for cc in range(8):
                v_op("tensor_tensor", pr5[:, cc],
                     qsl[:, cc].unsqueeze(3).broadcast_to([128, 16, 8, 8]),
                     ksl[:, cc].unsqueeze(2).broadcast_to([128, 16, 8, 8]),
                     ALU.mult)
            v_op("tensor_add", prods[:, 0:4096], prods[:, 0:4096],
                 prods[:, 4096:8192])
            v_op("tensor_add", prods[:, 0:2048], prods[:, 0:2048],
                 prods[:, 2048:4096])
            v_op("tensor_add", sbuf1, prods[:, 0:1024], prods[:, 1024:2048])
            softmax_av_write(1)
            pav = prods.rearrange("p (bb g a c) -> p bb g a c", bb=8, g=16, a=8)
            e4 = sexp.rearrange("p (g a bb) -> p g a bb", g=16, a=8)
            vsl = q4[:, :, :, VOFF:VOFF + 8]
            vb = vsl.transpose([0, 2, 1, 3])
            for bb2 in range(8):
                v_op("tensor_tensor", pav[:, bb2],
                     e4[:, :, :, bb2].unsqueeze(3).broadcast_to([128, 16, 8, 8]),
                     vb[:, bb2].unsqueeze(2).broadcast_to([128, 16, 8, 8]),
                     ALU.mult)
            v_op("tensor_add", prods[:, 0:4096], prods[:, 0:4096],
                 prods[:, 4096:8192])
            v_op("tensor_add", prods[:, 0:2048], prods[:, 0:2048],
                 prods[:, 2048:4096])
            v_op("tensor_add", oacc, prods[:, 0:1024], prods[:, 1024:2048])
            ol4 = olep.rearrange("p (g a c) -> p g a c", g=16, a=8)
            tap = s_taps.rearrange("p (br tp c) -> p br tp c", br=2, tp=3)
            pr4 = prods[:, 0:1024].rearrange("p (g a c) -> p g a c", g=16, a=8)
            v_op("tensor_tensor", ol4, vsl,
                 tap[:, 0, 1, :].unsqueeze(1).unsqueeze(2)
                 .broadcast_to([128, 16, 8, 8]), ALU.mult)
            v_op("tensor_tensor", pr4[:, :, 0:7, :], vsl[:, :, 0:7, :],
                 tap[:, 0, 0, :].unsqueeze(1).unsqueeze(2)
                 .broadcast_to([128, 16, 7, 8]), ALU.mult)
            v_op("tensor_add", ol4[:, :, 1:8, :], ol4[:, :, 1:8, :],
                 pr4[:, :, 0:7, :])
            v_op("tensor_tensor", pr4[:, :, 0:7, :], vsl[:, :, 1:8, :],
                 tap[:, 0, 2, :].unsqueeze(1).unsqueeze(2)
                 .broadcast_to([128, 16, 7, 8]), ALU.mult)
            v_op("tensor_add", ol4[:, :, 0:7, :], ol4[:, :, 0:7, :],
                 pr4[:, :, 0:7, :])
            oa4 = oacc.rearrange("p (g a c) -> p g a c", g=16, a=8)
            v_op("tensor_tensor", oa4, oa4,
                 srcp.rearrange("p (g a) -> p g a", g=16).unsqueeze(3)
                 .broadcast_to([128, 16, 8, 8]), ALU.mult)
            v_op("tensor_add", oacc, oacc, olep)
            of1 = ofull.rearrange("p (g a c) -> p g a c", g=16, a=8, c=16)[:, :, :, 0:8]
            v_op("tensor_tensor", of1, oa4,
                 s_cb[:, 0:8].unsqueeze(1).unsqueeze(2)
                 .broadcast_to([128, 16, 8, 8]), ALU.add)

            b2q_v = b2q.rearrange("p (c r) -> p c r", c=8)
            b2k_v = b2k.rearrange("p (c r) -> p c r", c=8)
            b2v_v = b2v.rearrange("p (c r) -> p c r", c=8)
            for oo in range(4):
                pr2 = prods[:, 0:2048].rearrange("p (c a q bb) -> p c a q bb", c=8, a=8, q=4)
                qs = b2q_v[:, :, 32 * oo:32 * oo + 32].rearrange(
                    "p c (a q) -> p c a q", a=8)
                ks = b2k_v[:, :, 32 * oo:32 * oo + 32].rearrange(
                    "p c (b q) -> p c b q", b=8)
                for cc in range(8):
                    v_op("tensor_tensor", pr2[:, cc],
                         qs[:, cc].unsqueeze(3).broadcast_to([128, 8, 4, 8]),
                         ks[:, cc].transpose([0, 2, 1]).unsqueeze(1)
                         .broadcast_to([128, 8, 4, 8]), ALU.mult)
                v_op("tensor_add", prods[:, 0:1024], prods[:, 0:1024],
                     prods[:, 1024:2048])
                v_op("tensor_add", prods[:, 0:512], prods[:, 0:512],
                     prods[:, 512:1024])
                v_op("tensor_add", sbuf1[:, 256 * oo:256 * (oo + 1)],
                     prods[:, 0:256], prods[:, 256:512])
            softmax_av_write(2)
            for oo in range(4):
                pav2 = prods[:, 0:2048].rearrange(
                    "p (bb a q c) -> p bb a q c", bb=8, a=8, q=4)
                es = sexp.rearrange("p (o a q bb) -> p o a q bb", o=4, a=8,
                                    q=4)[:, oo]
                vs = b2v_v[:, :, 32 * oo:32 * oo + 32].rearrange(
                    "p c (b q) -> p c b q", b=8)
                for bb2 in range(8):
                    v_op("tensor_tensor", pav2[:, bb2],
                         es[:, :, :, bb2].unsqueeze(3)
                         .broadcast_to([128, 8, 4, 8]),
                         vs[:, :, bb2, :].transpose([0, 2, 1]).unsqueeze(1)
                         .broadcast_to([128, 8, 4, 8]), ALU.mult)
                v_op("tensor_add", prods[:, 0:1024], prods[:, 0:1024],
                     prods[:, 1024:2048])
                v_op("tensor_add", prods[:, 0:512], prods[:, 0:512],
                     prods[:, 512:1024])
                v_op("tensor_add", oacc[:, 256 * oo:256 * (oo + 1)],
                     prods[:, 0:256], prods[:, 256:512])
            oa2 = oacc.rearrange("p (o a q c) -> p o a q c", o=4, a=8, q=4)
            rcp4 = srcp.rearrange("p (o a q) -> p o a q", o=4, a=8)
            for oo in range(4):
                v_op("tensor_tensor", oa2[:, oo], oa2[:, oo],
                     rcp4[:, oo].unsqueeze(3).broadcast_to([128, 8, 4, 8]),
                     ALU.mult)
            vall = b2v_v.rearrange("p c (o aa q) -> p c o aa q", o=4, aa=8)
            vper = vall.transpose([0, 2, 3, 4, 1])
            ol2 = olep.rearrange("p (o a q c) -> p o a q c", o=4, a=8, q=4)
            pr3 = prods[:, 0:1024].rearrange("p (o a q c) -> p o a q c", o=4, a=8, q=4)
            for oo in range(4):
                v_op("tensor_tensor", ol2[:, oo], vper[:, oo],
                     tap[:, 1, 1, :].unsqueeze(1).unsqueeze(2)
                     .broadcast_to([128, 8, 4, 8]), ALU.mult)
                v_op("tensor_tensor", pr3[:, oo, 0:7], vper[:, oo, 0:7],
                     tap[:, 1, 0, :].unsqueeze(1).unsqueeze(2)
                     .broadcast_to([128, 7, 4, 8]), ALU.mult)
                v_op("tensor_add", ol2[:, oo, 1:8], ol2[:, oo, 1:8],
                     pr3[:, oo, 0:7])
                v_op("tensor_tensor", pr3[:, oo, 0:7], vper[:, oo, 1:8],
                     tap[:, 1, 2, :].unsqueeze(1).unsqueeze(2)
                     .broadcast_to([128, 7, 4, 8]), ALU.mult)
                v_op("tensor_add", ol2[:, oo, 0:7], ol2[:, oo, 0:7],
                     pr3[:, oo, 0:7])
            v_op("tensor_add", oacc, oacc, olep)
            o2w = o2cm.rearrange("p (c rr) -> p c rr", c=8).rearrange(
                "p c (o a q) -> p c o a q", o=4, a=8).transpose([0, 2, 3, 4, 1])
            for oo in range(4):
                b2n = v_op("tensor_tensor", o2w[:, oo], oa2[:, oo],
                           s_cb[:, 8:16].unsqueeze(1).unsqueeze(2)
                           .broadcast_to([128, 8, 4, 8]), ALU.add)
            for c2 in range(8):
                W("tensor", chv, b2n if c2 == 0 else chv.n)
                PE("transpose", pwb[:, 256:384],
                   o2cm[:, 128 * c2:128 * (c2 + 1)], s_ident[:], inc=pe)
                W("vector", pe)
                ofc = ofull.rearrange("p (m c) -> p m c", c=16)[:, :, 8 + c2]
                v_op("tensor_copy", ofc, pwb[:, 256:384])

            for g in range(16):
                W("tensor", chv)
                PE("transpose", pwb[:, 384:512], ofull[:, 128 * g:128 * (g + 1)],
                   s_ident[:], inc=pe)
                W("vector", pe)
                v_op("tensor_copy", xtg, pwb[:, 384:512])
                xn = chv.n
                W("tensor", chv, xn)
                PE("matmul", pbig[:, 128 * g:128 * (g + 1)], xtg, s_bdproj[:],
                   start=True, stop=True, inc=(pe if g == 15 else None))
            W("vector", pe)
            pb16 = pbig.rearrange("p (m c) -> p m c", c=16)
            v_op("tensor_tensor", pb16, pb16,
                 s_pb[:].unsqueeze(1).broadcast_to([128, 128, 16]), ALU.add)
            v_op("tensor_reduce", m16.rearrange("p (m x) -> p m x", x=1), pb16,
                 AX, ALU.add)
            v_op("tensor_scalar", m16, m16, 1.0 / 16, None, ALU.mult)
            mn = chv.n
            W("scalar", chv, mn)
            a_op("activation", sqsb, pbig[:, :], ACTF.Square)
            sqn = cha.n
            W("vector", cha, sqn)
            v_op("tensor_reduce", v16.rearrange("p (m x) -> p m x", x=1),
                 sqsb.rearrange("p (m c) -> p m c", c=16), AX, ALU.add)
            v_op("tensor_scalar", v16, v16, 1.0 / 16, None, ALU.mult)
            v_op("tensor_mul", r16, m16, m16)
            v_op("tensor_sub", v16, v16, r16)
            vn2 = chv.n
            W("scalar", chv, vn2)
            a_op("activation", v16, v16, ACTF.Sqrt, bias=s_cst[:, 2:3])
            W("vector", cha)
            v_op("reciprocal", r16, v16)
            t4 = t16a.rearrange("p (m c) -> p m c", c=16)
            v_op("tensor_tensor", t4, pb16,
                 m16.unsqueeze(2).broadcast_to([128, 128, 16]), ALU.subtract)
            v_op("tensor_tensor", t4, t4,
                 r16.unsqueeze(2).broadcast_to([128, 128, 16]), ALU.mult)
            v_op("tensor_tensor", t4, t4,
                 s_lnw[:].unsqueeze(1).broadcast_to([128, 128, 16]), ALU.mult)
            v_op("tensor_tensor", t4, t4,
                 s_lnb[:].unsqueeze(1).broadcast_to([128, 128, 16]), ALU.add)
            t2w = tok2[sl].rearrange("p (c m) -> p c m", c=16).transpose([0, 2, 1])
            W("vector", pls[sl], pls_slot[sl])
            tn = v_op("tensor_tensor", t2w, t4,
                      ya.rearrange("p (m c) -> p m c", c=16), ALU.add)
            for _i in range(4):
                pe_slot_guard[_i] = chv.n
            pbig_act_guard[0] = cha.n
            W("sync", chv, tn)
            for c in range(16):
                DMA("sync", planes[c, b], tok2[sl][:, 128 * c:128 * (c + 1)],
                    pls[sl])
            pls_slot[sl] = pls[sl].n

        pend_s, pend_t = [], []

        def pump_w(b):
            if b < 6:
                for _ in range(32):
                    if scale_q:
                        pend_s.append(w_issue(scale_q.pop(0)))
                    while len(pend_s) > 1:
                        w_scale(pend_s.pop(0))
                if b == 5:
                    while pend_s:
                        w_scale(pend_s.pop(0))
            elif b == 6:
                w_finalize()
            else:
                n = 22 if b < NB - 1 else 64
                for _ in range(n):
                    if tern_q:
                        pend_t.append(w_issue(tern_q.pop(0)))
                    while len(pend_t) > 1:
                        w_tern(pend_t.pop(0))
                if b == NB - 1:
                    while pend_t:
                        w_tern(pend_t.pop(0))

        ttile = [B[:, 4096:6144], B[:, 6144:8192]]
        tps_sem = ldg
        tpw_sem = ldd

        def tp_tile(entry):
            widx, t = entry
            sl = t % 2
            tw0n, tw1n = tern_done[widx]
            W("sync", tw[0], tw0n)
            W("sync", tw[1], tw1n)
            W("sync", tpw_sem[sl])
            if widx < 3 - 1 + 0 and widx <= 1:
                terns = tern_g if widx == 0 else tern_u
                ternT = ternT_g if widx == 0 else ternT_u
                fc, ob = t % 16, t // 16
                DMA("sync", ttile[sl],
                    terns[2048 * ob:2048 * (ob + 1), 128 * fc:128 * (fc + 1)],
                    tps_sem[sl], transpose=True)
                W("sync", tps_sem[sl])
                dst = ternT[16 * ob:16 * (ob + 1), fc].transpose([1, 0, 2])
                DMA("sync", dst, ttile[sl].rearrange("p (j o) -> p j o", j=16),
                    tpw_sem[sl])
            else:
                DMA("sync", ttile[sl],
                    tern_d[:, 128 * t:128 * (t + 1)], tps_sem[sl],
                    transpose=True)
                W("sync", tps_sem[sl])
                DMA("sync", ternT_d[128 * t:128 * (t + 1), :], ttile[sl],
                    tpw_sem[sl])

        for b in range(NB):
            p1_batch(b)
            pump_w(b)
            if tp_jobs:
                for _ in range(24):
                    if tp_jobs and tp_jobs[0][0] in tern_done:
                        tp_tile(tp_jobs.pop(0))

        while tp_jobs:
            tp_tile(tp_jobs.pop(0))

        W("sync", pls[0])
        W("sync", pls[1])
        W("sync", tpw_sem[0])
        W("sync", tpw_sem[1])
        W("sync", tw[0])
        W("sync", tw[1])
        W("sync", chv)
        W("sync", cha)
        W("sync", chg)
        W("sync", pe)
        bd.em("sync", lambda e: e.sem_inc(bar.h, 1))
        bar.n += 1
        for eng in ("vector", "scalar", "gpsimd", "tensor"):
            W(eng, bar, bar.n)

        if dbg:
            W("gpsimd", chv)
            DMA("gpsimd", sdbg[:, 0:4], s_svec[:], ost[0])
            DMA("gpsimd", sdbg[:, 4:8], s_thvec[:], ost[0])
            DMA("gpsimd", sdbg[:, 8:12], s_wacc[:], ost[0])

        psum8 = [pbig[:, 0:512], pbig[:, 512:1024], pbig[:, 1024:1536],
                 pbig[:, 1536:2048], pq[:], pu[:], ph[:], pw[:]]
        guard8 = [chv.n] * 8
        slot_store = [0, 0, 0, 0]
        dd_guard = [0, 0]
        hm_guard = [0]
        wdt_pe_guard = [0, 0]
        ga_guard = [cha.n, cha.n]
        ost_cnt = [0]
        pwb = pw.bitcast(BF16)

        def quant_chunk(src_is_psum, src_ap, dst_ap, beta_bc, alpha_bc):
            v_op("tensor_tensor", qu[:], src_ap, beta_bc, ALU.mult)
            v_op("tensor_copy", qiq[:], qu[:])
            v_op("tensor_copy", qfq[:], qiq[:])
            dn = v_op("tensor_sub", dqb, qu[:], qfq[:])
            W("scalar", chv, dn)
            a_op("activation", zs1, dqb, ACTF.Sign, bias=s_cst[:, 0:1])
            s2n = a_op("activation", zs2, dqb, ACTF.Sign, bias=s_cst[:, 1:2])
            W("vector", cha, s2n)
            v_op("tensor_add", zzb, zs1, zs2)
            if alpha_bc is None:
                return v_op("scalar_tensor_tensor", dst_ap, zzb, 0.5, qfq[:],
                            ALU.mult, ALU.add)
            v_op("scalar_tensor_tensor", qu[:], zzb, 0.5, qfq[:], ALU.mult,
                 ALU.add)
            return v_op("tensor_tensor", dst_ap, qu[:], alpha_bc, ALU.mult)

        def maxabs_to_rows(nchunks, bufv, ncols, clipval, rows_dst):
            v_op("tensor_reduce", mxf[:].unsqueeze(2),
                 bufv[:, 0:nchunks, :].transpose([0, 2, 1]), AX, ALU.max,
                 apply_absolute_value=True)
            cn = v_op("tensor_copy", mxcast[:, 0:512], mxf[:])
            for ts in range(4):
                W("tensor", chv, cn if ts == 0 else chv.n)
                PE("transpose", pwb[:, 0:128],
                   mxcast[:, 128 * ts:128 * (ts + 1)], s_ident[:], inc=pe)
                W("vector", pe)
                v_op("tensor_reduce", s_mx2[:, ts:ts + 1], pwb[:, 0:128], AX,
                     ALU.max, apply_absolute_value=True)
            v_op("tensor_scalar_max", s_mx2[:], s_mx2[:], clipval)
            if rows_dst is not None:
                v_op("reciprocal", s_a2sd[:], s_mx2[:])
                v_op("tensor_scalar", s_stats[:, 8:12], s_a2sd[:], 127.0,
                     None, ALU.mult)
                cn2 = v_op("tensor_copy", stcast[:, 0:4], s_stats[:, 8:12])
                for ts in range(4):
                    W("tensor", chv, cn2 if ts == 0 else chv.n)
                    PE("transpose", pwb[0:1, 0:128],
                       stcast[:, ts:ts + 1], s_ident[:], inc=pe)
                    W("vector", pe)
                    v_op("tensor_copy", rows_dst[0:1, 128 * ts:128 * (ts + 1)],
                         pwb[0:1, 0:128])
                rn2 = chv.n
                W("tensor", chv, rn2)
                W("tensor", cha)
                PE("matmul", pq[:], s_ones[0:1, :], rows_dst[0:1, 0:512],
                   start=True, stop=True, inc=pe)
                W("vector", pe)
                v_op("tensor_copy", bc2, pq[:])
                return chv.n
            return chv.n

        def p2_supertile(sti):
            base = 4 * sti
            for ci in range(4):
                c = base + ci
                sl = c % 2
                W("sync", pe, wdt_pe_guard[sl])
                W("sync", chv, dd_guard[sl])
                DMA("sync", wdt[sl],
                    planes[c].rearrange("b (J q) m -> (b J) (q m)", J=8),
                    ldp[sl])
                DMA("sync", xf32b[:, sl, :], xres[c], ldr[sl])
                W("vector", ldp[sl])
                W("vector", ldr[sl])
                for _o in ost:
                    W("vector", _o)
                x1n = v_op("tensor_add", xf32a[:, ci % 2, :], xf32b[:, sl, :],
                           wdt[sl])
                dd_guard[sl] = chv.n
                v_op("tensor_reduce", s_stats[:, ci:ci + 1], xf32a[:, ci % 2, :],
                     AX, ALU.add)
                v_op("tensor_scalar", s_stats[:, ci:ci + 1],
                     s_stats[:, ci:ci + 1], 1.0 / D, None, ALU.mult)
                W("scalar", chv, x1n)
                a_op("activation", x1b[:, ci, :], xf32a[:, ci % 2, :],
                     ACTF.Square, accum_out=sg1[:])
                W("vector", cha)
                v_op("tensor_scalar", sg1[:], sg1[:], 1.0 / D, None, ALU.mult)
                v_op("tensor_mul", mu1[:], s_stats[:, ci:ci + 1],
                     s_stats[:, ci:ci + 1])
                v_op("tensor_sub", sg1[:], sg1[:], mu1[:])
                vn = chv.n
                W("scalar", chv, vn)
                a_op("activation", sg1[:], sg1[:], ACTF.Sqrt, bias=s_cst[:, 2:3])
                W("vector", cha)
                v_op("reciprocal", s_stats[:, 4 + ci:5 + ci], sg1[:])
                W("vector", cha)
                bn = v_op("tensor_copy", x1b[:, ci, :], xf32a[:, ci % 2, :])
                for fc in range(16):
                    W("tensor", chv, bn if fc == 0 else chv.n)
                    PE("transpose", pwb[:, 0:128],
                       x1b[:, ci, 128 * fc:128 * (fc + 1)], s_ident[:], inc=pe)
                    W("vector", pe)
                    v_op("tensor_copy", xq1[:, fc, 128 * ci:128 * (ci + 1)],
                         pwb[:, 0:128])
            for fc in range(16):
                W("tensor", chv)
                PE("matmul", ph[:], s_h128[:], xq1[:, fc, :], start=True,
                   stop=True, inc=pe)
                W("vector", pe)
                v_op("tensor_copy", xq1[:, fc, :], ph[:])
            for stg in range(4):
                step = 1 << stg
                for bb in range(0, 16, 2 * step):
                    for j in range(step):
                        Aap = xq1[:, bb + j, :]
                        Bap = xq1[:, bb + j + step, :]
                        v_op("tensor_add", Aap, Aap, Bap)
                        v_op("scalar_tensor_tensor", Bap, Bap, -2.0, Aap,
                             ALU.mult, ALU.add)
            v_op("tensor_scalar", fmisc[:, 0:4], s_stats[:, 0:4], float(D),
                 None, ALU.mult)
            pk0 = v_op("tensor_copy", stcast[:, 0:4], fmisc[:, 0:4])
            for ts in range(4):
                W("tensor", chv, pk0 if ts == 0 else chv.n)
                PE("transpose", pwb[0:1, 0:128],
                   stcast[:, ts:ts + 1], s_ident[:], inc=pe)
                W("vector", pe)
                v_op("tensor_copy",
                     strows[0:1, 128 * ts:128 * (ts + 1)], pwb[0:1, 0:128])
            v_op("tensor_sub", xq1[0:1, 0, :], xq1[0:1, 0, :],
                 strows[0:1, 0:512])
            maxabs_to_rows(16, xq1, 512, EPS1, None)
            v_op("reciprocal", s_a2sd[:], s_mx2[:])
            v_op("tensor_scalar", s_stats[:, 8:12], s_a2sd[:], 127.0, None,
                 ALU.mult)
            v_op("tensor_tensor", s_stats[:, 12:16], s_mx2[:],
                 s_stats[:, 4:8], ALU.mult)
            v_op("tensor_scalar", s_stats[:, 12:16], s_stats[:, 12:16],
                 C2K / 127.0, None, ALU.mult)
            v_op("tensor_copy", stcast[:, 4:8], s_stats[:, 8:12])
            pk = v_op("tensor_copy", stcast[:, 8:12], s_stats[:, 12:16])
            for q3 in range(2):
                for ts in range(4):
                    W("tensor", chv, pk if (q3 == 0 and ts == 0) else chv.n)
                    PE("transpose", pwb[0:1, 0:128],
                       stcast[:, 4 + 4 * q3 + ts:5 + 4 * q3 + ts],
                       s_ident[:], inc=pe)
                    W("vector", pe)
                    v_op("tensor_copy",
                         strows[0:1,
                                512 * (q3 + 1) + 128 * ts:512 * (q3 + 1) + 128 * (ts + 1)],
                         pwb[0:1, 0:128])
            W("tensor", chv)
            W("tensor", cha)
            PE("matmul", pq[:], s_ones[0:1, :], strows[0:1, 512:1024],
               start=True, stop=True, inc=pe)
            PE("matmul", pu[:], s_ones[0:1, :], strows[0:1, 1024:1536],
               start=True, stop=True, inc=pe)
            W("vector", pe)
            v_op("tensor_copy", bcb, pq[:])
            v_op("tensor_copy", bca, pu[:])
            for fc in range(16):
                quant_chunk(False, xq1[:, fc, :], xq1[:, fc, :], bcb, bca)
            xq_done = chv.n

            ldg_need = {}

            def wgu_ap(sl2):
                return B.rearrange("p (s b o) -> p s b o", s=2, b=64)[:, sl2]

            def load_og(og2, sl2):
                blk0 = 2 * og2 * 16
                src_g = ternT_g.rearrange("a b p o -> (a b) p o")[
                    blk0:blk0 + 32].transpose([1, 0, 2])
                src_u = ternT_u.rearrange("a b p o -> (a b) p o")[
                    blk0:blk0 + 32].transpose([1, 0, 2])
                DMA("sync", wgu_ap(sl2)[:, 0:32, :], src_g, ldg[sl2])
                DMA("scalar", wgu_ap(sl2)[:, 32:64, :], src_u, ldg[sl2])

            for og in range(32):
                sl = og % 2
                if og == 0:
                    load_og(0, 0)
                    ldg_need[0] = ldg[0].n
                if og + 1 < 32:
                    nsl = (og + 1) % 2
                    W("sync", pe)
                    W("scalar", pe)
                    load_og(og + 1, nsl)
                    ldg_need[og + 1] = ldg[nsl].n
                bd.em("tensor", (lambda s, v: (lambda e: e.wait_ge(s.h, v)))(
                    ldg[sl], ldg_need[og]))
                if og == 0:
                    bd.em("tensor", (lambda v: lambda e: e.wait_ge(chv.h, v))(
                        xq_done))
                for oc in range(2):
                    o = 2 * og + oc
                    pg_i, pu_i = 4, 5
                    wb = wgu_ap(sl)
                    bd.em("tensor", (lambda v: lambda e: e.wait_ge(cha.h, v))(
                        ga_guard[0]))
                    for fcc in range(16):
                        PE("matmul", psum8[pg_i], wb[:, oc * 16 + fcc, :],
                           xq1[:, fcc, :], start=(fcc == 0), stop=(fcc == 15),
                           inc=(pe if fcc == 15 else None))
                    bd.em("tensor", (lambda v: lambda e: e.wait_ge(cha.h, v))(
                        ga_guard[1]))
                    for fcc in range(16):
                        PE("matmul", psum8[pu_i], wb[:, 32 + oc * 16 + fcc, :],
                           xq1[:, fcc, :], start=(fcc == 0), stop=(fcc == 15),
                           inc=(pe if fcc == 15 else None))
                    W("scalar", pe)
                    W("scalar", chv, hm_guard[0])
                    a_op("activation", gsb, psum8[pg_i], ACTF.Silu,
                         scale=s_svec[:, 0:1])
                    ga_guard[0] = cha.n
                    a_op("activation", usb, psum8[pu_i], ACTF.Copy,
                         scale=s_svec[:, 1:2])
                    ga_guard[1] = cha.n
                    W("vector", cha)
                    v_op("tensor_mul", hbuf[:, o, :], gsb, usb)
                    hm_guard[0] = chv.n

            for stg in range(6):
                step = 1 << stg
                for bb in range(0, 64, 2 * step):
                    for j in range(step):
                        Aap = hbuf[:, bb + j, :]
                        Bap = hbuf[:, bb + j + step, :]
                        v_op("tensor_add", Aap, Aap, Bap)
                        v_op("scalar_tensor_tensor", Bap, Bap, -2.0, Aap,
                             ALU.mult, ALU.add)
            maxabs_to_rows(64, hbuf, 512, EPS2, strows)
            v_op("tensor_scalar", s_a2sd[:], s_mx2[:], C8K / 127.0, None,
                 ALU.mult)
            v_op("tensor_scalar", s_a2sd[:], s_a2sd[:], s_svec[:, 2:3], None,
                 ALU.mult)
            for o in range(64):
                W("tensor", chv)
                PE("matmul", ph[:], s_h128[:], hbuf[:, o, :], start=True,
                   stop=True, inc=pe)
                W("vector", pe)
                quant_chunk(True, ph[:], hbuf[:, o, :], bc2, None)
            q2_done = chv.n

            for foh in range(2):
                for ic in range(64):
                    sl = ic % 2
                    W("scalar", pe, wdt_pe_guard[sl])
                    W("scalar", chv, dd_guard[sl])
                    DMA("scalar", wdt[sl][0:128, 0:1024],
                        ternT_d[128 * ic:128 * (ic + 1),
                                1024 * foh:1024 * (foh + 1)], ldd[sl])
                    bd.em("tensor", (lambda s, v: lambda e: e.wait_ge(s.h, v))(
                        ldd[sl], ldd[sl].n))
                    if ic == 0:
                        bd.em("tensor", (lambda v: lambda e: e.wait_ge(chv.h, v))(
                            q2_done))
                        for bk in range(8):
                            bd.em("tensor",
                                  (lambda v: lambda e: e.wait_ge(chv.h, v))(
                                      guard8[bk]))
                    for ts in range(4):
                        for fo2 in range(2):
                            bk = ts * 2 + fo2
                            PE("matmul", psum8[bk],
                               hbuf[:, ic, 128 * ts:128 * (ts + 1)],
                               wdt[sl][:, 512 * fo2:512 * (fo2 + 1)],
                               start=(ic == 0), stop=(ic == 63),
                               inc=(pe if bk == 7 else None))
                    wdt_pe_guard[sl] = pe.n
                W("vector", pe)
                for ts in range(4):
                    c = base + ts
                    for fo2 in range(2):
                        bk = ts * 2 + fo2
                        col = 1024 * foh + 512 * fo2
                        stg_sl = (ts * 2 + fo2) % 4
                        stg_ap = xf32a.rearrange("p s n -> p (s n)")[
                            :, 512 * stg_sl:512 * (stg_sl + 1)]
                        W("vector", ost[stg_sl])
                        v_op("scalar_tensor_tensor", stg_ap, psum8[bk],
                             s_a2sd[:, ts:ts + 1], x1b[:, ts, col:col + 512],
                             ALU.mult, ALU.add)
                        guard8[bk] = chv.n
                        W("gpsimd", chv, chv.n)
                        DMA("gpsimd", out[c, :, col:col + 512], stg_ap,
                            ost[stg_sl])
                        ost_cnt[0] += 1

        for sti in range(4):
            p2_supertile(sti)

        for _o in ost:
            W("gpsimd", _o)
        if dbg:
            W("gpsimd", pls[0])
            W("gpsimd", pls[1])

        @block.sync
        def _(e):
            for f in bd.prog["sync"]:
                f(e)

        @block.vector
        def _(e):
            for f in bd.prog["vector"]:
                f(e)

        @block.scalar
        def _(e):
            for f in bd.prog["scalar"]:
                f(e)

        @block.gpsimd
        def _(e):
            for f in bd.prog["gpsimd"]:
                f(e)

        @block.tensor
        def _(e):
            for f in bd.prog["tensor"]:
                f(e)

    return nc


def _host_constants(qkv_w, proj_w, cw0, cb0, cw1, cb1, ln_w, ln_b, proj_b):
    bd_qkv = np.zeros((128, 384), np.float32)
    for t in range(8):
        bd_qkv[t * 16:(t + 1) * 16, t * 48:(t + 1) * 48] = qkv_w
    bd_proj = np.zeros((128, 128), np.float32)
    for t in range(8):
        bd_proj[t * 16:(t + 1) * 16, t * 16:(t + 1) * 16] = proj_w
    h128 = np.ascontiguousarray(_hadamard128().astype(ml_dtypes.bfloat16))
    ident = np.ascontiguousarray(np.eye(128, dtype=ml_dtypes.bfloat16))
    taps = np.zeros((1, 48), np.float32)
    for c in range(8):
        for tp in range(3):
            taps[0, 0 + tp * 8 + c] = cw0[c, 0, 1, tp]
            taps[0, 24 + tp * 8 + c] = cw1[c, 0, tp, 1]
    taps = np.ascontiguousarray(np.repeat(taps, 128, axis=0))
    cb = np.ascontiguousarray(
        np.concatenate([cb0, cb1]).reshape(1, 16).repeat(128, axis=0)
        .astype(np.float32))
    lnw = np.ascontiguousarray(
        np.asarray(ln_w, np.float32).reshape(1, 16).repeat(128, axis=0))
    lnb = np.ascontiguousarray(
        np.asarray(ln_b, np.float32).reshape(1, 16).repeat(128, axis=0))
    pb = np.ascontiguousarray(
        np.asarray(proj_b, np.float32).reshape(1, 16).repeat(128, axis=0))
    return dict(c_bdqkv=bd_qkv, c_bdproj=bd_proj, c_h128=h128, c_ident=ident,
                c_taps=taps, c_cb=cb, c_lnw=lnw, c_lnb=lnb, c_pb=pb)


def kernel(L0_attn_x, L0_attn_qkv_w, L0_attn_conv_w0, L0_attn_conv_b0,
           L0_attn_conv_w1, L0_attn_conv_b1, L0_attn_proj_w, L0_attn_proj_b,
           L0_attn_ln_w, L0_attn_ln_b, L0_ffn_w_gate, L0_ffn_w_up,
           L0_ffn_w_down):
    x = np.ascontiguousarray(np.asarray(L0_attn_x, np.float32)[0])
    consts = _host_constants(np.asarray(L0_attn_qkv_w, np.float32),
                             np.asarray(L0_attn_proj_w, np.float32),
                             np.asarray(L0_attn_conv_w0, np.float32),
                             np.asarray(L0_attn_conv_b0, np.float32),
                             np.asarray(L0_attn_conv_w1, np.float32),
                             np.asarray(L0_attn_conv_b1, np.float32),
                             np.asarray(L0_attn_ln_w, np.float32),
                             np.asarray(L0_attn_ln_b, np.float32),
                             np.asarray(L0_attn_proj_b, np.float32))
    wg = np.ascontiguousarray(np.asarray(L0_ffn_w_gate, np.float32))
    wu = np.ascontiguousarray(np.asarray(L0_ffn_w_up, np.float32))
    wd = np.ascontiguousarray(np.asarray(L0_ffn_w_down, np.float32))

    nc = build_nc()
    in_maps = []
    for k in range(8):
        xres_k = np.ascontiguousarray(
            np.stack([x[c * 1024 + 128 * k: c * 1024 + 128 * k + 128]
                      for c in range(NT)]))
        m = {"xs": np.ascontiguousarray(x[2048 * k:2048 * (k + 1)]),
             "xres": xres_k, "w_g": wg, "w_u": wu, "w_d": wd}
        m.update(consts)
        in_maps.append(m)
    res = run_bass_kernel_spmd(nc, in_maps, core_ids=list(range(8)))
    outp = np.empty((16384, 2048), np.float32)
    for k in range(8):
        ok = res.results[k]["out"]
        for c in range(NT):
            outp[c * 1024 + 128 * k: c * 1024 + 128 * k + 128] = ok[c]
    return outp.reshape(1, 16384, 2048)


# version 09 (final)
# speedup vs baseline: 1.3623x; 1.0710x over previous
import numpy as np
import ml_dtypes
import concourse.bass as bass
import concourse.mybir as mybir
from concourse.bass_utils import run_bass_kernel_spmd

F32 = mybir.dt.float32
BF16 = mybir.dt.bfloat16
I32 = mybir.dt.int32
ALU = mybir.AluOpType
ACTF = mybir.ActivationFunctionType
AX = mybir.AxisListType.X

R = 2048
D = 2048
NB = 16
NT = 16
I8K = 8192
EPS = 1e-5
C2K = float(1.0 / np.sqrt(np.float64(2048.0)))
C8K = float(1.0 / np.sqrt(np.float64(8192.0)))
SCL8 = float(1.0 / np.sqrt(np.float64(8.0)))
EPS1 = float(EPS / C2K)
EPS2 = float(EPS / C8K)

QOFF, KOFF, VOFF = 0, 16, 32


def _hadamard128():
    h = np.ones((1, 1), np.float32)
    while h.shape[0] < 128:
        h = np.block([[h, h], [h, -h]])
    return h


class Sem:
    def __init__(self, h):
        self.h = h
        self.n = 0


class Builder:

    def __init__(self, nc):
        self.nc = nc
        self.prog = {k: [] for k in ("sync", "vector", "scalar", "gpsimd", "tensor")}

    def em(self, eng, fn):
        self.prog[eng].append(fn)

    def chain_op(self, eng, chain, fn_name, *a, **kw):
        n0 = chain.n
        import traceback
        site = traceback.extract_stack()[-3]

        def f(e):
            if n0:
                e.wait_ge(chain.h, n0)
            try:
                getattr(e, fn_name)(*a, **kw).then_inc(chain.h, 1)
            except Exception as ex:
                raise RuntimeError(
                    f"{fn_name} from {site.filename}:{site.lineno}: {ex}"
                ) from ex

        self.em(eng, f)
        chain.n += 1
        return chain.n

    def wait(self, eng, sem, val=None):
        v = sem.n if val is None else val
        if v > 0:
            self.em(eng, lambda e: e.wait_ge(sem.h, v))

    def pe(self, fn_name, *a, inc=None, **kw):

        def f(e):
            inst = getattr(e, fn_name)(*a, **kw)
            if inc is not None:
                inst.then_inc(inc.h, 1)

        self.em("tensor", f)
        if inc is not None:
            inc.n += 1
            return inc.n

    def dma(self, eng, out_ap, in_ap, sem, transpose=False):
        def f(e):
            if transpose:
                inst = e.dma_start_transpose(out_ap, in_ap)
            else:
                inst = e.dma_start(out=out_ap, in_=in_ap)
            inst.then_inc(sem.h, 16)

        self.em(eng, f)
        sem.n += 16
        return sem.n


def build_nc(dbg=False):
    nc = bass.Bass("TRN2", target_bir_lowering=False, debug=False)

    xs = nc.dram_tensor("xs", [R, D], F32, kind="ExternalInput").ap()
    xres = nc.dram_tensor("xres", [NT, 128, D], F32, kind="ExternalInput").ap()
    w_g = nc.dram_tensor("w_g", [I8K, D], F32, kind="ExternalInput").ap()
    w_u = nc.dram_tensor("w_u", [I8K, D], F32, kind="ExternalInput").ap()
    w_d = nc.dram_tensor("w_d", [D, I8K], F32, kind="ExternalInput").ap()
    c_bdqkv = nc.dram_tensor("c_bdqkv", [128, 384], F32, kind="ExternalInput").ap()
    c_bdproj = nc.dram_tensor("c_bdproj", [128, 128], F32, kind="ExternalInput").ap()
    c_h128 = nc.dram_tensor("c_h128", [128, 128], BF16, kind="ExternalInput").ap()
    c_ident = nc.dram_tensor("c_ident", [128, 128], BF16, kind="ExternalInput").ap()
    c_taps = nc.dram_tensor("c_taps", [128, 48], F32, kind="ExternalInput").ap()
    c_cb = nc.dram_tensor("c_cb", [128, 16], F32, kind="ExternalInput").ap()
    c_lnw = nc.dram_tensor("c_lnw", [128, 16], F32, kind="ExternalInput").ap()
    c_lnb = nc.dram_tensor("c_lnb", [128, 16], F32, kind="ExternalInput").ap()
    c_pb = nc.dram_tensor("c_pb", [128, 16], F32, kind="ExternalInput").ap()

    kind_i = "ExternalOutput" if dbg else "Internal"
    tern_g = nc.dram_tensor("tern_g", [I8K, D], BF16, kind=kind_i).ap()
    tern_u = nc.dram_tensor("tern_u", [I8K, D], BF16, kind="Internal").ap()
    tern_d = nc.dram_tensor("tern_d", [D, I8K], BF16, kind="Internal").ap()
    ternT_g = nc.dram_tensor("ternT_g", [64, 16, 128, 128], BF16, kind="Internal").ap()
    ternT_u = nc.dram_tensor("ternT_u", [64, 16, 128, 128], BF16, kind="Internal").ap()
    ternT_d = nc.dram_tensor("ternT_d", [I8K, D], BF16, kind="Internal").ap()
    planes = nc.dram_tensor("planes", [NT, NB, 128, 128], BF16, kind=kind_i).ap()
    sdbg = nc.dram_tensor("sdbg", [128, 12], F32, kind=kind_i).ap() if dbg else None
    out = nc.dram_tensor("out", [NT, 128, D], F32, kind="ExternalOutput").ap()

    import contextlib
    with contextlib.ExitStack() as _stack:
        _E = _stack.enter_context
        s_bdqkv = _E(nc.sbuf_tensor("s_bdqkv", [128, 384], BF16))
        s_bdproj = _E(nc.sbuf_tensor("s_bdproj", [128, 128], BF16))
        s_h128 = _E(nc.sbuf_tensor("s_h128", [128, 128], BF16))
        s_ident = _E(nc.sbuf_tensor("s_ident", [128, 128], BF16))
        s_taps = _E(nc.sbuf_tensor("s_taps", [128, 48], F32))
        s_cb = _E(nc.sbuf_tensor("s_cb", [128, 16], F32))
        s_lnw = _E(nc.sbuf_tensor("s_lnw", [128, 16], F32))
        s_lnb = _E(nc.sbuf_tensor("s_lnb", [128, 16], F32))
        s_pb = _E(nc.sbuf_tensor("s_pb", [128, 16], F32))
        s_cst = _E(nc.sbuf_tensor("s_cst", [128, 4], F32))
        s_ones = _E(nc.sbuf_tensor("s_ones", [128, 128], BF16))
        s_onesf = _E(nc.sbuf_tensor("s_onesf", [128, 1], F32))
        s_onesg = _E(nc.sbuf_tensor("s_onesg", [128, 128], F32))
        s_svec = _E(nc.sbuf_tensor("s_svec", [128, 4], F32))
        s_thvec = _E(nc.sbuf_tensor("s_thvec", [128, 4], F32))
        s_wacc = _E(nc.sbuf_tensor("s_wacc", [128, 4], F32))
        s_stats = _E(nc.sbuf_tensor("s_stats", [128, 16], F32))
        s_mx2 = _E(nc.sbuf_tensor("s_mx2", [128, 4], F32))
        s_a2sd = _E(nc.sbuf_tensor("s_a2sd", [128, 4], F32))
        s_rows = _E(nc.sbuf_tensor("s_rows", [128, 2592], BF16))
        xf32a = _E(nc.sbuf_tensor("xf32a", [128, 2, 2048], F32))
        xf32b = _E(nc.sbuf_tensor("xf32b", [128, 2, 2048], F32))
        fmisc = _E(nc.sbuf_tensor("fmisc", [128, 160], F32))
        f128 = _E(nc.sbuf_tensor("f128", [128, 768], F32))
        mu1 = _E(nc.sbuf_tensor("mu1", [128, 1], F32))
        sg1 = _E(nc.sbuf_tensor("sg1", [128, 1], F32))
        qu = _E(nc.sbuf_tensor("qu", [128, 512], F32))
        qiq = _E(nc.sbuf_tensor("qiq", [128, 512], I32))
        qfq = _E(nc.sbuf_tensor("qfq", [128, 512], F32))
        qub = _E(nc.sbuf_tensor("qub", [128, 512], F32))
        qib = _E(nc.sbuf_tensor("qib", [128, 512], I32))
        qfb = _E(nc.sbuf_tensor("qfb", [128, 512], F32))
        qs2 = _E(nc.sbuf_tensor("qs2", [128, 2048], BF16))
        gu2 = _E(nc.sbuf_tensor("gu2", [128, 1024], BF16))
        mxf = _E(nc.sbuf_tensor("mxf", [128, 512], F32))
        A = _E(nc.sbuf_tensor("A", [128, 32768], BF16))
        B = _E(nc.sbuf_tensor("B", [128, 16384], BF16))
        C = _E(nc.sbuf_tensor("C", [128, 8192], BF16))
        DD = _E(nc.sbuf_tensor("DD", [128, 4096], BF16))
        x1b = _E(nc.sbuf_tensor("x1b", [128, 4, 2048], BF16))
        sm = _E(nc.sbuf_tensor("sm", [128, 4096], BF16))
        pbig = _E(nc.psum_tensor("pbig", [128, 2048], F32))
        pq = _E(nc.psum_tensor("pq", [128, 512], F32))
        pu = _E(nc.psum_tensor("pu", [128, 512], F32))
        ph = _E(nc.psum_tensor("ph", [128, 512], F32))
        pw = _E(nc.psum_tensor("pw", [128, 512], F32))
        chv_h = _E(nc.semaphore("chv"))
        cha_h = _E(nc.semaphore("cha"))
        chg_h = _E(nc.semaphore("chg"))
        pe_h = _E(nc.semaphore("pe0"))
        ldc_h = _E(nc.semaphore("ldc"))
        ldx0_h = _E(nc.semaphore("ldx0"))
        ldx1_h = _E(nc.semaphore("ldx1"))
        ldw0_h = _E(nc.semaphore("ldw0"))
        ldw1_h = _E(nc.semaphore("ldw1"))
        tw0_h = _E(nc.semaphore("tw0"))
        tw1_h = _E(nc.semaphore("tw1"))
        pls_h0 = _E(nc.semaphore("pls0"))
        pls_h1 = _E(nc.semaphore("pls1"))
        ldg0_h = _E(nc.semaphore("ldg0"))
        ldg1_h = _E(nc.semaphore("ldg1"))
        ldd0_h = _E(nc.semaphore("ldd0"))
        ldd1_h = _E(nc.semaphore("ldd1"))
        ldp0_h = _E(nc.semaphore("ldp0"))
        ldp1_h = _E(nc.semaphore("ldp1"))
        ldr0_h = _E(nc.semaphore("ldr0"))
        ldr1_h = _E(nc.semaphore("ldr1"))
        ost_h0 = _E(nc.semaphore("ost0"))
        ost_h1 = _E(nc.semaphore("ost1"))
        ost_h2 = _E(nc.semaphore("ost2"))
        ost_h3 = _E(nc.semaphore("ost3"))
        bar_h = _E(nc.semaphore("bar"))
        block = _E(nc.Block())
        bd = Builder(nc)
        chv, cha, chg, pe = Sem(chv_h), Sem(cha_h), Sem(chg_h), Sem(pe_h)
        ldc = Sem(ldc_h)
        ldx = [Sem(ldx0_h), Sem(ldx1_h)]
        ldw = [Sem(ldw0_h), Sem(ldw1_h)]
        tw = [Sem(tw0_h), Sem(tw1_h)]
        pls = [Sem(pls_h0), Sem(pls_h1)]
        ldg = [Sem(ldg0_h), Sem(ldg1_h)]
        ldd = [Sem(ldd0_h), Sem(ldd1_h)]
        ldp = [Sem(ldp0_h), Sem(ldp1_h)]
        ldr = [Sem(ldr0_h), Sem(ldr1_h)]
        ost = [Sem(ost_h0), Sem(ost_h1), Sem(ost_h2), Sem(ost_h3)]
        bar = Sem(bar_h)

        def v_op(fn, *a, **k):
            return bd.chain_op("vector", chv, fn, *a, **k)

        def a_op(fn, *a, **k):
            return bd.chain_op("scalar", cha, fn, *a, **k)

        def g_op(fn, *a, **k):
            return bd.chain_op("gpsimd", chg, fn, *a, **k)

        W = bd.wait
        PE = bd.pe
        DMA = bd.dma

        def a_slice(start, n):
            return A[:, start:start + n]

        prods = a_slice(0, 8192)
        qkvs = a_slice(8192, 6144)
        sbuf1 = a_slice(14336, 1024)
        sexp = a_slice(15360, 1024)
        oacc = a_slice(16384, 1024)
        olep = a_slice(17408, 1024)
        o2cm = a_slice(18432, 1024)
        ofull = a_slice(19456, 2048)
        t16a = a_slice(21504, 2048)
        b2q = a_slice(23552, 1024)
        b2k = a_slice(24576, 1024)
        b2v = a_slice(25600, 1024)
        ya = a_slice(26624, 2048)
        sqsb = a_slice(28672, 2048)
        tok2 = [B[:, 0:2048], B[:, 2048:4096]]
        tstr = [DD[:, 0:2048], DD[:, 2048:4096]]
        sgn1 = x1b[:, 3, :]
        xtg = C[:, 0:128]
        hbuf = A.rearrange("p (c n) -> p c n", n=512)
        xq1 = C.rearrange("p (c n) -> p c n", n=512)
        wgu = B.rearrange("p (s gu f o) -> p s gu f o", s=2, gu=2, f=16)
        wdt = [DD[:, 0:2048], DD[:, 2048:4096]]
        smax = f128[:, 0:128]
        smaxs = f128[:, 128:256]
        ssum = f128[:, 256:384]
        srcp = f128[:, 384:512]
        m16 = f128[:, 512:640]
        v16 = f128[:, 640:768]
        r16 = fmisc[:, 0:128]
        gsb = sm[:, 0:512]
        usb = sm[:, 512:1024]
        gsb2 = [gsb, gu2[:, 0:512]]
        usb2 = [usb, gu2[:, 512:1024]]
        dqb = sm[:, 1024:1536]
        zs1 = sm[:, 1536:2048]
        zs2 = sm[:, 2048:2560]
        zzb = sm[:, 2560:3072]
        bca = sm[:, 3072:3584]
        bcb = sm[:, 3584:4096]
        qsc = [
            (qu[:], qiq[:], qfq[:], dqb, zs1, zs2, zzb),
            (qub[:], qib[:], qfb[:], qs2[:, 0:512], qs2[:, 512:1024],
             qs2[:, 1024:1536], qs2[:, 1536:2048]),
        ]
        bc2 = s_rows[:, 1536:2048]
        strows = s_rows[:, 0:1536]
        mxcast = s_rows[:, 2048:2560]
        stcast = s_rows[:, 2560:2592]

        DMA("sync", s_h128[:], c_h128, ldc)
        DMA("sync", s_ident[:], c_ident, ldc)
        DMA("sync", s_taps[:], c_taps, ldc)
        DMA("sync", s_cb[:], c_cb, ldc)
        DMA("sync", s_lnw[:], c_lnw, ldc)
        DMA("sync", s_lnb[:], c_lnb, ldc)
        DMA("sync", s_pb[:], c_pb, ldc)
        DMA("sync", xf32a[:, 0, 0:384], c_bdqkv, ldc)
        DMA("sync", xf32a[:, 0, 384:512], c_bdproj, ldc)
        W("vector", ldc)
        v_op("tensor_copy", s_bdqkv[:], xf32a[:, 0, 0:384])
        v_op("tensor_copy", s_bdproj[:], xf32a[:, 0, 384:512])
        v_op("memset", s_cst[:, 0:1], -0.5)
        v_op("memset", s_cst[:, 1:2], 0.5)
        v_op("memset", s_cst[:, 2:3], EPS)
        v_op("memset", s_ones[:], 1.0)
        v_op("memset", s_onesf[:], 1.0)
        v_op("memset", s_onesg[:], 1.0)
        v_op("memset", s_wacc[:], 0.0)
        consts_v = chv.n

        W("sync", chv, consts_v)
        DMA("sync", xf32a[:, 0, :], xs[0:128, :], ldx[0])

        wg_v = w_g.rearrange("(c p) d -> c p d", p=128)
        wu_v = w_u.rearrange("(c p) d -> c p d", p=128)
        def wd_v_f(ce):
            c0, e0 = ce // 4, ce % 4
            return w_d[128 * c0:128 * (c0 + 1), 2048 * e0:2048 * (e0 + 1)]
        tg_v = tern_g.rearrange("(c p) d -> c p d", p=128)
        tu_v = tern_u.rearrange("(c p) d -> c p d", p=128)
        def td_v_f(ce):
            c0, e0 = ce // 4, ce % 4
            return tern_d[128 * c0:128 * (c0 + 1), 2048 * e0:2048 * (e0 + 1)]

        scale_q = ([(wg_v[c], 0) for c in range(64)] +
                   [(wu_v[c], 1) for c in range(64)] +
                   [(wd_v_f(c), 2) for c in range(64)])
        tern_q = ([(wg_v[c], tg_v[c], 0) for c in range(64)] +
                  [(wu_v[c], tu_v[c], 1) for c in range(64)] +
                  [(wd_v_f(c), td_v_f(c), 2) for c in range(64)])
        wslot = [0]
        wslot_guard = [None, None]
        sgn_guard = [0]
        svec_ready = [0]
        tern_left = [64, 64, 64]
        tern_done = {}
        tp_jobs = [(widx, t) for widx in range(3) for t in range(64)]

        def w_issue(job):
            sl = wslot[0]
            wslot[0] ^= 1
            if wslot_guard[sl] is not None:
                W("sync", wslot_guard[sl][0], wslot_guard[sl][1])
            DMA("sync", xf32b[:, sl, :], job[0], ldw[sl])
            return (sl, job)

        scale_flip = [0]
        acc_guard = [0]

        def w_scale(entry):
            sl, (src, widx) = entry
            scale_flip[0] ^= 1
            if scale_flip[0]:
                W("vector", ldw[sl])
                v_op("tensor_reduce", s_wacc[:, 3:4], xf32b[:, sl, :], AX,
                     ALU.add, apply_absolute_value=True)
                v_op("tensor_add", s_wacc[:, widx:widx + 1],
                     s_wacc[:, widx:widx + 1], s_wacc[:, 3:4])
                wslot_guard[sl] = (chv, chv.n)
            else:
                W("scalar", ldw[sl])
                W("scalar", chv, acc_guard[0])
                a_op("activation", sgn1, xf32b[:, sl, :], ACTF.Abs,
                     accum_out=s_stats[:, 15:16])
                an = cha.n
                W("vector", cha, an)
                v_op("tensor_add", s_wacc[:, widx:widx + 1],
                     s_wacc[:, widx:widx + 1], s_stats[:, 15:16])
                acc_guard[0] = chv.n
                wslot_guard[sl] = (cha, an)

        def w_tern(entry):
            sl, (src, dst, widx) = entry
            W("scalar", ldw[sl])
            W("scalar", tw[sl])
            W("scalar", chv, max(sgn_guard[0], svec_ready[0]))
            a_op("activation", sgn1, xf32b[:, sl, :], ACTF.Sign,
                 bias=s_thvec[:, widx:widx + 1])
            a_op("activation", tstr[sl], xf32b[:, sl, :], ACTF.Sign,
                 bias=s_svec[:, widx:widx + 1])
            s2n = cha.n
            wslot_guard[sl] = (cha, cha.n)
            W("vector", cha, s2n)
            zn = v_op("tensor_add", tstr[sl], tstr[sl], sgn1)
            sgn_guard[0] = zn
            W("gpsimd", chv, zn)
            DMA("gpsimd", dst, tstr[sl], tw[sl])
            tern_left[widx] -= 1
            if tern_left[widx] == 0:
                tern_done[widx] = (tw[0].n, tw[1].n)

        def w_finalize():
            W("tensor", chv)
            W("tensor", cha)
            for widx in range(3):
                PE("matmul", pw[0:1, widx:widx + 1], s_onesf[:, 0:1],
                   s_wacc[:, widx:widx + 1], start=True, stop=True,
                   inc=(pe if widx == 2 else None))
            W("vector", pe)
            v_op("tensor_copy", s_wacc[0:1, 0:3], pw[0:1, 0:3])
            sn = chv.n
            W("tensor", chv, sn)
            PE("matmul", pw[:, 4:7], s_onesg[0:1, :], s_wacc[0:1, 0:3],
               start=True, stop=True, inc=pe)
            W("vector", pe)
            v_op("tensor_scalar", s_svec[:, 0:3], pw[:, 4:7],
                 0.5 / float(I8K * D), None, ALU.mult)
            v_op("tensor_scalar", s_thvec[:, 0:3], s_svec[:, 0:3], -1.0, None,
                 ALU.mult)
            svec_ready[0] = chv.n

        pe_slot_guard = [0, 0, 0, 0]
        tr_guard = [(chv_h, 0), (chv_h, 0), (chv_h, 0), (chv_h, 0)]
        xtg_guard = [0, 0]
        pbig_act_guard = [0]
        pls_slot = [0, 0]

        def p1_batch(b):
            sl = b % 2
            if b + 1 < NB:
                DMA("sync", xf32a[:, (b + 1) % 2, :],
                    xs[128 * (b + 1):128 * (b + 2), :], ldx[(b + 1) % 2])
            xt = xf32a[:, sl, :]
            xw = 16 * (b // 2 + 1)
            W("vector", ldx[sl], xw)
            W("scalar", ldx[sl], xw)
            v_op("tensor_reduce", mu1[:], xt, AX, ALU.add)
            a_op("activation", sqsb, xt, ACTF.Square, accum_out=sg1[:])
            v_op("tensor_scalar", mu1[:], mu1[:], 1.0 / D, None, ALU.mult)
            W("vector", cha)
            v_op("tensor_scalar", sg1[:], sg1[:], 1.0 / D, None, ALU.mult)
            v_op("tensor_mul", m16[:, 0:1], mu1[:], mu1[:])
            v_op("tensor_sub", sg1[:], sg1[:], m16[:, 0:1])
            vn = chv.n
            W("scalar", chv, vn)
            a_op("activation", sg1[:], sg1[:], ACTF.Sqrt, bias=s_cst[:, 2:3])
            W("vector", cha)
            v_op("reciprocal", sg1[:], sg1[:])
            yn = v_op("tensor_scalar", ya, xt, mu1[:, 0:1], sg1[:, 0:1],
                      ALU.subtract, ALU.mult)

            pwb = pw.bitcast(BF16)
            phb = ph.bitcast(BF16)
            tbank = [phb[:, 0:128], pwb[:, 0:128]]
            xtg2 = [xtg, C[:, 128:256]]
            for g in range(16):
                sl = g % 2
                bd.em("tensor", (lambda s2, v: lambda e: e.wait_ge(s2, v))(
                    tr_guard[sl][0], tr_guard[sl][1]))
                if g == 0:
                    W("tensor", chv, yn)
                PE("transpose", tbank[sl], ya[:, 128 * g:128 * (g + 1)],
                   s_ident[:], inc=pe)
                tev = pe.n
                use_act = (g % 4) >= 2
                if use_act:
                    W("scalar", pe, max(tev, xtg_guard[sl]))
                    a_op("activation", xtg2[sl], tbank[sl], ACTF.Copy)
                    tr_guard[sl] = (cha.h, cha.n)
                    bd.em("tensor", (lambda v: lambda e: e.wait_ge(cha.h, v))(
                        cha.n))
                else:
                    W("vector", pe, max(tev, xtg_guard[sl]))
                    v_op("tensor_copy", xtg2[sl], tbank[sl])
                    tr_guard[sl] = (chv.h, chv.n)
                    bd.em("tensor", (lambda v: lambda e: e.wait_ge(chv.h, v))(
                        chv.n))
                W("tensor", chv, pe_slot_guard[g % 4])
                bd.em("tensor", (lambda v: lambda e: e.wait_ge(cha.h, v))(
                    pbig_act_guard[0]))
                PE("matmul", pbig[:, 512 * (g % 4):512 * (g % 4) + 384],
                   xtg2[sl], s_bdqkv[:], start=True, stop=True, inc=pe)
                xtg_guard[sl] = pe.n
                W("vector", pe)
                v_op("tensor_copy", qkvs[:, 384 * g:384 * (g + 1)],
                     pbig[:, 512 * (g % 4):512 * (g % 4) + 384])
                pe_slot_guard[g % 4] = chv.n

            tbank2 = [phb[:, 256:384], pwb[:, 256:384]]
            items = [(off, dst, c2) for off, dst in
                     ((QOFF, b2q), (KOFF, b2k), (VOFF, b2v)) for c2 in range(8)]
            qkv_ready = chv.n
            for i3, (off, dst, c2) in enumerate(items):
                sl = i3 % 2
                srcap = qkvs[:, off + 8 + c2::48]
                bd.em("tensor", (lambda s2, v: lambda e: e.wait_ge(s2, v))(
                    tr_guard[2 + sl][0], tr_guard[2 + sl][1]))
                if i3 == 0:
                    W("tensor", chv, qkv_ready)
                PE("transpose", tbank2[sl], srcap, s_ident[:], inc=pe)
                tev = pe.n
                if i3 % 4 >= 2:
                    W("scalar", pe, tev)
                    a_op("activation", dst[:, 128 * c2:128 * (c2 + 1)],
                         tbank2[sl], ACTF.Copy)
                    tr_guard[2 + sl] = (cha.h, cha.n)
                else:
                    W("vector", pe, tev)
                    v_op("tensor_copy", dst[:, 128 * c2:128 * (c2 + 1)],
                         tbank2[sl])
                    tr_guard[2 + sl] = (chv.h, chv.n)

            q4 = qkvs.rearrange("p (g t co) -> p g t co", g=16, co=48)

            def softmax_av_write(kind):
                v_op("tensor_reduce",
                     smax.rearrange("p (w x) -> p w x", x=1),
                     sbuf1.rearrange("p (w bb) -> p w bb", bb=8), AX, ALU.max)
                v_op("tensor_scalar", smaxs, smax, SCL8, None, ALU.mult)
                sn = v_op("scalar_tensor_tensor",
                          sbuf1.rearrange("p (w bb) -> p w bb", bb=8),
                          sbuf1.rearrange("p (w bb) -> p w bb", bb=8), SCL8,
                          smaxs.unsqueeze(2).broadcast_to([128, 128, 8]),
                          ALU.mult, ALU.subtract)
                W("scalar", chv, sn)
                en = a_op("activation", sexp, sbuf1, ACTF.Exp)
                W("vector", cha, en)
                v_op("tensor_reduce",
                     ssum.rearrange("p (w x) -> p w x", x=1),
                     sexp.rearrange("p (w bb) -> p w bb", bb=8), AX, ALU.add)
                v_op("reciprocal", srcp, ssum)

            pr5 = prods.rearrange("p (c g a bb) -> p c g a bb", c=8, g=16, a=8)
            qsl = q4[:, :, :, QOFF:QOFF + 8].transpose([0, 3, 1, 2])
            ksl = q4[:, :, :, KOFF:KOFF + 8].transpose([0, 3, 1, 2])
            for cc in range(8):
                v_op("tensor_tensor", pr5[:, cc],
                     qsl[:, cc].unsqueeze(3).broadcast_to([128, 16, 8, 8]),
                     ksl[:, cc].unsqueeze(2).broadcast_to([128, 16, 8, 8]),
                     ALU.mult)
            v_op("tensor_add", prods[:, 0:4096], prods[:, 0:4096],
                 prods[:, 4096:8192])
            v_op("tensor_add", prods[:, 0:2048], prods[:, 0:2048],
                 prods[:, 2048:4096])
            v_op("tensor_add", sbuf1, prods[:, 0:1024], prods[:, 1024:2048])
            softmax_av_write(1)
            pav = prods.rearrange("p (bb g a c) -> p bb g a c", bb=8, g=16, a=8)
            e4 = sexp.rearrange("p (g a bb) -> p g a bb", g=16, a=8)
            vsl = q4[:, :, :, VOFF:VOFF + 8]
            vb = vsl.transpose([0, 2, 1, 3])
            for bb2 in range(8):
                v_op("tensor_tensor", pav[:, bb2],
                     e4[:, :, :, bb2].unsqueeze(3).broadcast_to([128, 16, 8, 8]),
                     vb[:, bb2].unsqueeze(2).broadcast_to([128, 16, 8, 8]),
                     ALU.mult)
            v_op("tensor_add", prods[:, 0:4096], prods[:, 0:4096],
                 prods[:, 4096:8192])
            v_op("tensor_add", prods[:, 0:2048], prods[:, 0:2048],
                 prods[:, 2048:4096])
            v_op("tensor_add", oacc, prods[:, 0:1024], prods[:, 1024:2048])
            ol4 = olep.rearrange("p (g a c) -> p g a c", g=16, a=8)
            tap = s_taps.rearrange("p (br tp c) -> p br tp c", br=2, tp=3)
            pr4 = prods[:, 0:1024].rearrange("p (g a c) -> p g a c", g=16, a=8)
            v_op("tensor_tensor", ol4, vsl,
                 tap[:, 0, 1, :].unsqueeze(1).unsqueeze(2)
                 .broadcast_to([128, 16, 8, 8]), ALU.mult)
            v_op("tensor_tensor", pr4[:, :, 0:7, :], vsl[:, :, 0:7, :],
                 tap[:, 0, 0, :].unsqueeze(1).unsqueeze(2)
                 .broadcast_to([128, 16, 7, 8]), ALU.mult)
            v_op("tensor_add", ol4[:, :, 1:8, :], ol4[:, :, 1:8, :],
                 pr4[:, :, 0:7, :])
            v_op("tensor_tensor", pr4[:, :, 0:7, :], vsl[:, :, 1:8, :],
                 tap[:, 0, 2, :].unsqueeze(1).unsqueeze(2)
                 .broadcast_to([128, 16, 7, 8]), ALU.mult)
            v_op("tensor_add", ol4[:, :, 0:7, :], ol4[:, :, 0:7, :],
                 pr4[:, :, 0:7, :])
            oa4 = oacc.rearrange("p (g a c) -> p g a c", g=16, a=8)
            v_op("tensor_tensor", oa4, oa4,
                 srcp.rearrange("p (g a) -> p g a", g=16).unsqueeze(3)
                 .broadcast_to([128, 16, 8, 8]), ALU.mult)
            v_op("tensor_add", oacc, oacc, olep)
            of1 = ofull.rearrange("p (g a c) -> p g a c", g=16, a=8, c=16)[:, :, :, 0:8]
            v_op("tensor_tensor", of1, oa4,
                 s_cb[:, 0:8].unsqueeze(1).unsqueeze(2)
                 .broadcast_to([128, 16, 8, 8]), ALU.add)

            b2q_v = b2q.rearrange("p (c r) -> p c r", c=8)
            b2k_v = b2k.rearrange("p (c r) -> p c r", c=8)
            b2v_v = b2v.rearrange("p (c r) -> p c r", c=8)
            for oo in range(4):
                pr2 = prods[:, 0:2048].rearrange("p (c a q bb) -> p c a q bb", c=8, a=8, q=4)
                qs = b2q_v[:, :, 32 * oo:32 * oo + 32].rearrange(
                    "p c (a q) -> p c a q", a=8)
                ks = b2k_v[:, :, 32 * oo:32 * oo + 32].rearrange(
                    "p c (b q) -> p c b q", b=8)
                for cc in range(8):
                    v_op("tensor_tensor", pr2[:, cc],
                         qs[:, cc].unsqueeze(3).broadcast_to([128, 8, 4, 8]),
                         ks[:, cc].transpose([0, 2, 1]).unsqueeze(1)
                         .broadcast_to([128, 8, 4, 8]), ALU.mult)
                v_op("tensor_add", prods[:, 0:1024], prods[:, 0:1024],
                     prods[:, 1024:2048])
                v_op("tensor_add", prods[:, 0:512], prods[:, 0:512],
                     prods[:, 512:1024])
                v_op("tensor_add", sbuf1[:, 256 * oo:256 * (oo + 1)],
                     prods[:, 0:256], prods[:, 256:512])
            softmax_av_write(2)
            for oo in range(4):
                pav2 = prods[:, 0:2048].rearrange(
                    "p (bb a q c) -> p bb a q c", bb=8, a=8, q=4)
                es = sexp.rearrange("p (o a q bb) -> p o a q bb", o=4, a=8,
                                    q=4)[:, oo]
                vs = b2v_v[:, :, 32 * oo:32 * oo + 32].rearrange(
                    "p c (b q) -> p c b q", b=8)
                for bb2 in range(8):
                    v_op("tensor_tensor", pav2[:, bb2],
                         es[:, :, :, bb2].unsqueeze(3)
                         .broadcast_to([128, 8, 4, 8]),
                         vs[:, :, bb2, :].transpose([0, 2, 1]).unsqueeze(1)
                         .broadcast_to([128, 8, 4, 8]), ALU.mult)
                v_op("tensor_add", prods[:, 0:1024], prods[:, 0:1024],
                     prods[:, 1024:2048])
                v_op("tensor_add", prods[:, 0:512], prods[:, 0:512],
                     prods[:, 512:1024])
                v_op("tensor_add", oacc[:, 256 * oo:256 * (oo + 1)],
                     prods[:, 0:256], prods[:, 256:512])
            oa2 = oacc.rearrange("p (o a q c) -> p o a q c", o=4, a=8, q=4)
            rcp4 = srcp.rearrange("p (o a q) -> p o a q", o=4, a=8)
            for oo in range(4):
                v_op("tensor_tensor", oa2[:, oo], oa2[:, oo],
                     rcp4[:, oo].unsqueeze(3).broadcast_to([128, 8, 4, 8]),
                     ALU.mult)
            vall = b2v_v.rearrange("p c (o aa q) -> p c o aa q", o=4, aa=8)
            vper = vall.transpose([0, 2, 3, 4, 1])
            ol2 = olep.rearrange("p (o a q c) -> p o a q c", o=4, a=8, q=4)
            pr3 = prods[:, 0:1024].rearrange("p (o a q c) -> p o a q c", o=4, a=8, q=4)
            for oo in range(4):
                v_op("tensor_tensor", ol2[:, oo], vper[:, oo],
                     tap[:, 1, 1, :].unsqueeze(1).unsqueeze(2)
                     .broadcast_to([128, 8, 4, 8]), ALU.mult)
                v_op("tensor_tensor", pr3[:, oo, 0:7], vper[:, oo, 0:7],
                     tap[:, 1, 0, :].unsqueeze(1).unsqueeze(2)
                     .broadcast_to([128, 7, 4, 8]), ALU.mult)
                v_op("tensor_add", ol2[:, oo, 1:8], ol2[:, oo, 1:8],
                     pr3[:, oo, 0:7])
                v_op("tensor_tensor", pr3[:, oo, 0:7], vper[:, oo, 1:8],
                     tap[:, 1, 2, :].unsqueeze(1).unsqueeze(2)
                     .broadcast_to([128, 7, 4, 8]), ALU.mult)
                v_op("tensor_add", ol2[:, oo, 0:7], ol2[:, oo, 0:7],
                     pr3[:, oo, 0:7])
            v_op("tensor_add", oacc, oacc, olep)
            o2w = o2cm.rearrange("p (c rr) -> p c rr", c=8).rearrange(
                "p c (o a q) -> p c o a q", o=4, a=8).transpose([0, 2, 3, 4, 1])
            for oo in range(4):
                b2n = v_op("tensor_tensor", o2w[:, oo], oa2[:, oo],
                           s_cb[:, 8:16].unsqueeze(1).unsqueeze(2)
                           .broadcast_to([128, 8, 4, 8]), ALU.add)
            for c2 in range(8):
                sl = c2 % 2
                bd.em("tensor", (lambda s2, v: lambda e: e.wait_ge(s2, v))(
                    tr_guard[2 + sl][0], tr_guard[2 + sl][1]))
                if c2 == 0:
                    W("tensor", chv, b2n)
                PE("transpose", tbank2[sl],
                   o2cm[:, 128 * c2:128 * (c2 + 1)], s_ident[:], inc=pe)
                tev = pe.n
                ofc = ofull.rearrange("p (m c) -> p m c", c=16)[:, :, 8 + c2]
                if c2 % 4 >= 2:
                    W("scalar", pe, tev)
                    a_op("activation", ofc, tbank2[sl], ACTF.Copy)
                    tr_guard[2 + sl] = (cha.h, cha.n)
                else:
                    W("vector", pe, tev)
                    v_op("tensor_copy", ofc, tbank2[sl])
                    tr_guard[2 + sl] = (chv.h, chv.n)

            of_ready = chv.n
            of_ready_a = cha.n
            for g in range(16):
                sl = g % 2
                bd.em("tensor", (lambda s2, v: lambda e: e.wait_ge(s2, v))(
                    tr_guard[sl][0], tr_guard[sl][1]))
                if g == 0:
                    W("tensor", chv, of_ready)
                    bd.em("tensor", (lambda v: lambda e: e.wait_ge(cha.h, v))(
                        of_ready_a))
                PE("transpose", tbank[sl], ofull[:, 128 * g:128 * (g + 1)],
                   s_ident[:], inc=pe)
                tev = pe.n
                if g % 4 >= 2:
                    W("scalar", pe, max(tev, xtg_guard[sl]))
                    a_op("activation", xtg2[sl], tbank[sl], ACTF.Copy)
                    tr_guard[sl] = (cha.h, cha.n)
                    bd.em("tensor", (lambda v: lambda e: e.wait_ge(cha.h, v))(
                        cha.n))
                else:
                    W("vector", pe, max(tev, xtg_guard[sl]))
                    v_op("tensor_copy", xtg2[sl], tbank[sl])
                    tr_guard[sl] = (chv.h, chv.n)
                    bd.em("tensor", (lambda v: lambda e: e.wait_ge(chv.h, v))(
                        chv.n))
                PE("matmul", pbig[:, 128 * g:128 * (g + 1)], xtg2[sl],
                   s_bdproj[:], start=True, stop=True, inc=pe)
                xtg_guard[sl] = pe.n
            W("vector", pe)
            pb16 = pbig.rearrange("p (m c) -> p m c", c=16)
            v_op("tensor_tensor", pb16, pb16,
                 s_pb[:].unsqueeze(1).broadcast_to([128, 128, 16]), ALU.add)
            v_op("tensor_reduce", m16.rearrange("p (m x) -> p m x", x=1), pb16,
                 AX, ALU.add)
            v_op("tensor_scalar", m16, m16, 1.0 / 16, None, ALU.mult)
            mn = chv.n
            W("scalar", chv, mn)
            a_op("activation", sqsb, pbig[:, :], ACTF.Square)
            sqn = cha.n
            W("vector", cha, sqn)
            v_op("tensor_reduce", v16.rearrange("p (m x) -> p m x", x=1),
                 sqsb.rearrange("p (m c) -> p m c", c=16), AX, ALU.add)
            v_op("tensor_scalar", v16, v16, 1.0 / 16, None, ALU.mult)
            v_op("tensor_mul", r16, m16, m16)
            v_op("tensor_sub", v16, v16, r16)
            vn2 = chv.n
            W("scalar", chv, vn2)
            a_op("activation", v16, v16, ACTF.Sqrt, bias=s_cst[:, 2:3])
            W("vector", cha)
            v_op("reciprocal", r16, v16)
            t4 = t16a.rearrange("p (m c) -> p m c", c=16)
            v_op("tensor_tensor", t4, pb16,
                 m16.unsqueeze(2).broadcast_to([128, 128, 16]), ALU.subtract)
            v_op("tensor_tensor", t4, t4,
                 r16.unsqueeze(2).broadcast_to([128, 128, 16]), ALU.mult)
            v_op("tensor_tensor", t4, t4,
                 s_lnw[:].unsqueeze(1).broadcast_to([128, 128, 16]), ALU.mult)
            v_op("tensor_tensor", t4, t4,
                 s_lnb[:].unsqueeze(1).broadcast_to([128, 128, 16]), ALU.add)
            t2w = tok2[sl].rearrange("p (c m) -> p c m", c=16).transpose([0, 2, 1])
            W("vector", pls[sl], pls_slot[sl])
            tn = v_op("tensor_tensor", t2w, t4,
                      ya.rearrange("p (m c) -> p m c", c=16), ALU.add)
            for _i in range(4):
                pe_slot_guard[_i] = chv.n
            pbig_act_guard[0] = cha.n
            W("sync", chv, tn)
            for c in range(16):
                DMA("sync", planes[c, b], tok2[sl][:, 128 * c:128 * (c + 1)],
                    pls[sl])
            pls_slot[sl] = pls[sl].n

        pend_s, pend_t = [], []

        def pump_w(b):
            if b < 6:
                for _ in range(32):
                    if scale_q:
                        pend_s.append(w_issue(scale_q.pop(0)))
                    while len(pend_s) > 1:
                        w_scale(pend_s.pop(0))
                if b == 5:
                    while pend_s:
                        w_scale(pend_s.pop(0))
            elif b == 6:
                w_finalize()
            else:
                n = 22 if b < NB - 1 else 64
                for _ in range(n):
                    if tern_q:
                        pend_t.append(w_issue(tern_q.pop(0)))
                    while len(pend_t) > 1:
                        w_tern(pend_t.pop(0))
                if b == NB - 1:
                    while pend_t:
                        w_tern(pend_t.pop(0))

        ttile = [B[:, 4096:6144], B[:, 6144:8192]]
        tps_sem = ldg
        tpw_sem = ldd

        def tp_tile(entry):
            widx, t = entry
            sl = t % 2
            tw0n, tw1n = tern_done[widx]
            W("sync", tw[0], tw0n)
            W("sync", tw[1], tw1n)
            W("sync", tpw_sem[sl])
            if widx < 3 - 1 + 0 and widx <= 1:
                terns = tern_g if widx == 0 else tern_u
                ternT = ternT_g if widx == 0 else ternT_u
                fc, ob = t % 16, t // 16
                DMA("sync", ttile[sl],
                    terns[2048 * ob:2048 * (ob + 1), 128 * fc:128 * (fc + 1)],
                    tps_sem[sl], transpose=True)
                W("sync", tps_sem[sl])
                dst = ternT[16 * ob:16 * (ob + 1), fc].transpose([1, 0, 2])
                DMA("sync", dst, ttile[sl].rearrange("p (j o) -> p j o", j=16),
                    tpw_sem[sl])
            else:
                DMA("sync", ttile[sl],
                    tern_d[:, 128 * t:128 * (t + 1)], tps_sem[sl],
                    transpose=True)
                W("sync", tps_sem[sl])
                DMA("sync", ternT_d[128 * t:128 * (t + 1), :], ttile[sl],
                    tpw_sem[sl])

        for b in range(NB):
            p1_batch(b)
            pump_w(b)
            if tp_jobs:
                for _ in range(24):
                    if tp_jobs and tp_jobs[0][0] in tern_done:
                        tp_tile(tp_jobs.pop(0))

        while tp_jobs:
            tp_tile(tp_jobs.pop(0))

        W("sync", pls[0])
        W("sync", pls[1])
        W("sync", tpw_sem[0])
        W("sync", tpw_sem[1])
        W("sync", tw[0])
        W("sync", tw[1])
        W("sync", chv)
        W("sync", cha)
        W("sync", chg)
        W("sync", pe)
        bd.em("sync", lambda e: e.sem_inc(bar.h, 1))
        bar.n += 1
        for eng in ("vector", "scalar", "gpsimd", "tensor"):
            W(eng, bar, bar.n)

        if dbg:
            W("gpsimd", chv)
            DMA("gpsimd", sdbg[:, 0:4], s_svec[:], ost[0])
            DMA("gpsimd", sdbg[:, 4:8], s_thvec[:], ost[0])
            DMA("gpsimd", sdbg[:, 8:12], s_wacc[:], ost[0])

        psum8 = [pbig[:, 0:512], pbig[:, 512:1024], pbig[:, 1024:1536],
                 pbig[:, 1536:2048], pq[:], pu[:], ph[:], pw[:]]
        guard8 = [chv.n] * 8
        slot_store = [0, 0, 0, 0]
        dd_guard = [0, 0]
        hm_guard = [0, 0]
        pe_oc = [0, 0]
        uev_guard = [0, 0]
        hq_pe = [0, 0]
        uq_mark = [0]
        wdt_pe_guard = [0, 0]
        ga_guard = [cha.n] * 4
        ost_cnt = [0]
        pwb = pw.bitcast(BF16)

        def quant_chunk(src_is_psum, src_ap, dst_ap, beta_bc, alpha_bc,
                        par=0):
            quq, qiq2, qfq2, dq2, z1, z2, zz2 = qsc[par]
            v_op("tensor_tensor", quq, src_ap, beta_bc, ALU.mult)
            uq_mark[0] = chv.n
            v_op("tensor_copy", qiq2, quq)
            v_op("tensor_copy", qfq2, qiq2)
            dn = v_op("tensor_sub", dq2, quq, qfq2)
            W("scalar", chv, dn)
            a_op("activation", z1, dq2, ACTF.Sign, bias=s_cst[:, 0:1])
            s2n = a_op("activation", z2, dq2, ACTF.Sign, bias=s_cst[:, 1:2])
            W("vector", cha, s2n)
            v_op("tensor_add", zz2, z1, z2)
            if alpha_bc is None:
                return v_op("scalar_tensor_tensor", dst_ap, zz2, 0.5, qfq2,
                            ALU.mult, ALU.add)
            v_op("scalar_tensor_tensor", quq, zz2, 0.5, qfq2, ALU.mult,
                 ALU.add)
            return v_op("tensor_tensor", dst_ap, quq, alpha_bc, ALU.mult)

        def maxabs_to_rows(nchunks, bufv, ncols, clipval, rows_dst):
            v_op("tensor_reduce", mxf[:].unsqueeze(2),
                 bufv[:, 0:nchunks, :].transpose([0, 2, 1]), AX, ALU.max,
                 apply_absolute_value=True)
            cn = v_op("tensor_copy", mxcast[:, 0:512], mxf[:])
            for ts in range(4):
                W("tensor", chv, cn if ts == 0 else chv.n)
                PE("transpose", pwb[:, 0:128],
                   mxcast[:, 128 * ts:128 * (ts + 1)], s_ident[:], inc=pe)
                W("vector", pe)
                v_op("tensor_reduce", s_mx2[:, ts:ts + 1], pwb[:, 0:128], AX,
                     ALU.max, apply_absolute_value=True)
            v_op("tensor_scalar_max", s_mx2[:], s_mx2[:], clipval)
            if rows_dst is not None:
                v_op("reciprocal", s_a2sd[:], s_mx2[:])
                v_op("tensor_scalar", s_stats[:, 8:12], s_a2sd[:], 127.0,
                     None, ALU.mult)
                cn2 = v_op("tensor_copy", stcast[:, 0:4], s_stats[:, 8:12])
                for ts in range(4):
                    W("tensor", chv, cn2 if ts == 0 else chv.n)
                    PE("transpose", pwb[0:1, 0:128],
                       stcast[:, ts:ts + 1], s_ident[:], inc=pe)
                    W("vector", pe)
                    v_op("tensor_copy", rows_dst[0:1, 128 * ts:128 * (ts + 1)],
                         pwb[0:1, 0:128])
                rn2 = chv.n
                W("tensor", chv, rn2)
                W("tensor", cha)
                PE("matmul", pq[:], s_ones[0:1, :], rows_dst[0:1, 0:512],
                   start=True, stop=True, inc=pe)
                W("vector", pe)
                v_op("tensor_copy", bc2, pq[:])
                return chv.n
            return chv.n

        def p2_supertile(sti):
            base = 4 * sti
            for ci in range(4):
                c = base + ci
                sl = c % 2
                W("sync", pe, wdt_pe_guard[sl])
                W("sync", chv, dd_guard[sl])
                DMA("sync", wdt[sl],
                    planes[c].rearrange("b (J q) m -> (b J) (q m)", J=8),
                    ldp[sl])
                DMA("sync", xf32b[:, sl, :], xres[c], ldr[sl])
                W("vector", ldp[sl])
                W("vector", ldr[sl])
                for _o in ost:
                    W("vector", _o)
                x1n = v_op("tensor_add", xf32a[:, ci % 2, :], xf32b[:, sl, :],
                           wdt[sl])
                dd_guard[sl] = chv.n
                v_op("tensor_reduce", s_stats[:, ci:ci + 1], xf32a[:, ci % 2, :],
                     AX, ALU.add)
                v_op("tensor_scalar", s_stats[:, ci:ci + 1],
                     s_stats[:, ci:ci + 1], 1.0 / D, None, ALU.mult)
                W("scalar", chv, x1n)
                a_op("activation", x1b[:, ci, :], xf32a[:, ci % 2, :],
                     ACTF.Square, accum_out=sg1[:])
                W("vector", cha)
                v_op("tensor_scalar", sg1[:], sg1[:], 1.0 / D, None, ALU.mult)
                v_op("tensor_mul", mu1[:], s_stats[:, ci:ci + 1],
                     s_stats[:, ci:ci + 1])
                v_op("tensor_sub", sg1[:], sg1[:], mu1[:])
                vn = chv.n
                W("scalar", chv, vn)
                a_op("activation", sg1[:], sg1[:], ACTF.Sqrt, bias=s_cst[:, 2:3])
                W("vector", cha)
                v_op("reciprocal", s_stats[:, 4 + ci:5 + ci], sg1[:])
                W("vector", cha)
                bn = v_op("tensor_copy", x1b[:, ci, :], xf32a[:, ci % 2, :])
                for fc in range(16):
                    W("tensor", chv, bn if fc == 0 else chv.n)
                    PE("transpose", pwb[:, 0:128],
                       x1b[:, ci, 128 * fc:128 * (fc + 1)], s_ident[:], inc=pe)
                    W("vector", pe)
                    v_op("tensor_copy", xq1[:, fc, 128 * ci:128 * (ci + 1)],
                         pwb[:, 0:128])
            for fc in range(16):
                W("tensor", chv)
                PE("matmul", ph[:], s_h128[:], xq1[:, fc, :], start=True,
                   stop=True, inc=pe)
                W("vector", pe)
                v_op("tensor_copy", xq1[:, fc, :], ph[:])
            for stg in range(4):
                step = 1 << stg
                for bb in range(0, 16, 2 * step):
                    for j in range(step):
                        Aap = xq1[:, bb + j, :]
                        Bap = xq1[:, bb + j + step, :]
                        v_op("tensor_add", Aap, Aap, Bap)
                        v_op("scalar_tensor_tensor", Bap, Bap, -2.0, Aap,
                             ALU.mult, ALU.add)
            v_op("tensor_scalar", fmisc[:, 0:4], s_stats[:, 0:4], float(D),
                 None, ALU.mult)
            pk0 = v_op("tensor_copy", stcast[:, 0:4], fmisc[:, 0:4])
            for ts in range(4):
                W("tensor", chv, pk0 if ts == 0 else chv.n)
                PE("transpose", pwb[0:1, 0:128],
                   stcast[:, ts:ts + 1], s_ident[:], inc=pe)
                W("vector", pe)
                v_op("tensor_copy",
                     strows[0:1, 128 * ts:128 * (ts + 1)], pwb[0:1, 0:128])
            v_op("tensor_sub", xq1[0:1, 0, :], xq1[0:1, 0, :],
                 strows[0:1, 0:512])
            maxabs_to_rows(16, xq1, 512, EPS1, None)
            v_op("reciprocal", s_a2sd[:], s_mx2[:])
            v_op("tensor_scalar", s_stats[:, 8:12], s_a2sd[:], 127.0, None,
                 ALU.mult)
            v_op("tensor_tensor", s_stats[:, 12:16], s_mx2[:],
                 s_stats[:, 4:8], ALU.mult)
            v_op("tensor_scalar", s_stats[:, 12:16], s_stats[:, 12:16],
                 C2K / 127.0, None, ALU.mult)
            v_op("tensor_copy", stcast[:, 4:8], s_stats[:, 8:12])
            pk = v_op("tensor_copy", stcast[:, 8:12], s_stats[:, 12:16])
            for q3 in range(2):
                for ts in range(4):
                    W("tensor", chv, pk if (q3 == 0 and ts == 0) else chv.n)
                    PE("transpose", pwb[0:1, 0:128],
                       stcast[:, 4 + 4 * q3 + ts:5 + 4 * q3 + ts],
                       s_ident[:], inc=pe)
                    W("vector", pe)
                    v_op("tensor_copy",
                         strows[0:1,
                                512 * (q3 + 1) + 128 * ts:512 * (q3 + 1) + 128 * (ts + 1)],
                         pwb[0:1, 0:128])
            W("tensor", chv)
            W("tensor", cha)
            PE("matmul", pq[:], s_ones[0:1, :], strows[0:1, 512:1024],
               start=True, stop=True, inc=pe)
            PE("matmul", pu[:], s_ones[0:1, :], strows[0:1, 1024:1536],
               start=True, stop=True, inc=pe)
            W("vector", pe)
            v_op("tensor_copy", bcb, pq[:])
            v_op("tensor_copy", bca, pu[:])
            for fc in range(16):
                quant_chunk(False, xq1[:, fc, :], xq1[:, fc, :], bcb, bca)
            xq_done = chv.n

            ldg_need = {}

            def wgu_ap(sl2):
                return B.rearrange("p (s b o) -> p s b o", s=2, b=64)[:, sl2]

            def load_og(og2, sl2):
                blk0 = 2 * og2 * 16
                src_g = ternT_g.rearrange("a b p o -> (a b) p o")[
                    blk0:blk0 + 32].transpose([1, 0, 2])
                src_u = ternT_u.rearrange("a b p o -> (a b) p o")[
                    blk0:blk0 + 32].transpose([1, 0, 2])
                DMA("sync", wgu_ap(sl2)[:, 0:32, :], src_g, ldg[sl2])
                DMA("scalar", wgu_ap(sl2)[:, 32:64, :], src_u, ldg[sl2])

            for og in range(32):
                sl = og % 2
                if og == 0:
                    load_og(0, 0)
                    ldg_need[0] = ldg[0].n
                if og + 1 < 32:
                    nsl = (og + 1) % 2
                    W("sync", pe)
                    W("scalar", pe)
                    load_og(og + 1, nsl)
                    ldg_need[og + 1] = ldg[nsl].n
                bd.em("tensor", (lambda s, v: (lambda e: e.wait_ge(s.h, v)))(
                    ldg[sl], ldg_need[og]))
                if og == 0:
                    bd.em("tensor", (lambda v: lambda e: e.wait_ge(chv.h, v))(
                        xq_done))
                for oc in range(2):
                    o = 2 * og + oc
                    par = o % 2
                    pg_i, pu_i = (4, 5) if par == 0 else (6, 7)
                    wb = wgu_ap(sl)
                    bd.em("tensor", (lambda v: lambda e: e.wait_ge(cha.h, v))(
                        ga_guard[2 * par]))
                    for fcc in range(16):
                        PE("matmul", psum8[pg_i], wb[:, oc * 16 + fcc, :],
                           xq1[:, fcc, :], start=(fcc == 0), stop=(fcc == 15),
                           inc=(pe if fcc == 15 else None))
                    bd.em("tensor", (lambda v: lambda e: e.wait_ge(cha.h, v))(
                        ga_guard[2 * par + 1]))
                    for fcc in range(16):
                        PE("matmul", psum8[pu_i], wb[:, 32 + oc * 16 + fcc, :],
                           xq1[:, fcc, :], start=(fcc == 0), stop=(fcc == 15),
                           inc=(pe if fcc == 15 else None))
                    pe_oc[par] = pe.n
                    W("scalar", pe, pe_oc[par])
                    W("scalar", chv, hm_guard[par])
                    a_op("activation", gsb2[par], psum8[pg_i], ACTF.Silu,
                         scale=s_svec[:, 0:1])
                    ga_guard[2 * par] = cha.n
                    a_op("activation", usb2[par], psum8[pu_i], ACTF.Copy,
                         scale=s_svec[:, 1:2])
                    ga_guard[2 * par + 1] = cha.n
                    W("vector", cha)
                    v_op("tensor_mul", hbuf[:, o, :], gsb2[par], usb2[par])
                    hm_guard[par] = chv.n

            for stg in range(6):
                step = 1 << stg
                for bb in range(0, 64, 2 * step):
                    for j in range(step):
                        Aap = hbuf[:, bb + j, :]
                        Bap = hbuf[:, bb + j + step, :]
                        v_op("tensor_add", Aap, Aap, Bap)
                        v_op("scalar_tensor_tensor", Bap, Bap, -2.0, Aap,
                             ALU.mult, ALU.add)
            maxabs_to_rows(64, hbuf, 512, EPS2, strows)
            v_op("tensor_scalar", s_a2sd[:], s_mx2[:], C8K / 127.0, None,
                 ALU.mult)
            v_op("tensor_scalar", s_a2sd[:], s_a2sd[:], s_svec[:, 2:3], None,
                 ALU.mult)
            hq_banks = [ph, pw]
            for o in range(64):
                par = o % 2
                bank = hq_banks[par]
                bd.em("tensor", (lambda v: lambda e: e.wait_ge(chv.h, v))(
                    uev_guard[par]))
                PE("matmul", bank[:], s_h128[:], hbuf[:, o, :], start=True,
                   stop=True, inc=pe)
                hq_pe[par] = pe.n
                W("vector", pe, hq_pe[par])
                quant_chunk(True, bank[:], hbuf[:, o, :], bc2, None,
                            par=par)
                uev_guard[par] = uq_mark[0]
            q2_done = chv.n

            for foh in range(2):
                for ic in range(64):
                    sl = ic % 2
                    W("scalar", pe, wdt_pe_guard[sl])
                    W("scalar", chv, dd_guard[sl])
                    DMA("scalar", wdt[sl][0:128, 0:1024],
                        ternT_d[128 * ic:128 * (ic + 1),
                                1024 * foh:1024 * (foh + 1)], ldd[sl])
                    bd.em("tensor", (lambda s, v: lambda e: e.wait_ge(s.h, v))(
                        ldd[sl], ldd[sl].n))
                    if ic == 0:
                        bd.em("tensor", (lambda v: lambda e: e.wait_ge(chv.h, v))(
                            q2_done))
                        for bk in range(8):
                            bd.em("tensor",
                                  (lambda v: lambda e: e.wait_ge(chv.h, v))(
                                      guard8[bk]))
                    for ts in range(4):
                        for fo2 in range(2):
                            bk = ts * 2 + fo2
                            PE("matmul", psum8[bk],
                               hbuf[:, ic, 128 * ts:128 * (ts + 1)],
                               wdt[sl][:, 512 * fo2:512 * (fo2 + 1)],
                               start=(ic == 0), stop=(ic == 63),
                               inc=(pe if bk == 7 else None))
                    wdt_pe_guard[sl] = pe.n
                W("vector", pe)
                for ts in range(4):
                    c = base + ts
                    for fo2 in range(2):
                        bk = ts * 2 + fo2
                        col = 1024 * foh + 512 * fo2
                        stg_sl = (ts * 2 + fo2) % 4
                        stg_ap = xf32a.rearrange("p s n -> p (s n)")[
                            :, 512 * stg_sl:512 * (stg_sl + 1)]
                        W("vector", ost[stg_sl])
                        v_op("scalar_tensor_tensor", stg_ap, psum8[bk],
                             s_a2sd[:, ts:ts + 1], x1b[:, ts, col:col + 512],
                             ALU.mult, ALU.add)
                        guard8[bk] = chv.n
                        W("gpsimd", chv, chv.n)
                        DMA("gpsimd", out[c, :, col:col + 512], stg_ap,
                            ost[stg_sl])
                        ost_cnt[0] += 1

        for sti in range(4):
            p2_supertile(sti)

        for _o in ost:
            W("gpsimd", _o)
        if dbg:
            W("gpsimd", pls[0])
            W("gpsimd", pls[1])

        @block.sync
        def _(e):
            for f in bd.prog["sync"]:
                f(e)

        @block.vector
        def _(e):
            for f in bd.prog["vector"]:
                f(e)

        @block.scalar
        def _(e):
            for f in bd.prog["scalar"]:
                f(e)

        @block.gpsimd
        def _(e):
            for f in bd.prog["gpsimd"]:
                f(e)

        @block.tensor
        def _(e):
            for f in bd.prog["tensor"]:
                f(e)

    return nc


def _host_constants(qkv_w, proj_w, cw0, cb0, cw1, cb1, ln_w, ln_b, proj_b):
    bd_qkv = np.zeros((128, 384), np.float32)
    for t in range(8):
        bd_qkv[t * 16:(t + 1) * 16, t * 48:(t + 1) * 48] = qkv_w
    bd_proj = np.zeros((128, 128), np.float32)
    for t in range(8):
        bd_proj[t * 16:(t + 1) * 16, t * 16:(t + 1) * 16] = proj_w
    h128 = np.ascontiguousarray(_hadamard128().astype(ml_dtypes.bfloat16))
    ident = np.ascontiguousarray(np.eye(128, dtype=ml_dtypes.bfloat16))
    taps = np.zeros((1, 48), np.float32)
    for c in range(8):
        for tp in range(3):
            taps[0, 0 + tp * 8 + c] = cw0[c, 0, 1, tp]
            taps[0, 24 + tp * 8 + c] = cw1[c, 0, tp, 1]
    taps = np.ascontiguousarray(np.repeat(taps, 128, axis=0))
    cb = np.ascontiguousarray(
        np.concatenate([cb0, cb1]).reshape(1, 16).repeat(128, axis=0)
        .astype(np.float32))
    lnw = np.ascontiguousarray(
        np.asarray(ln_w, np.float32).reshape(1, 16).repeat(128, axis=0))
    lnb = np.ascontiguousarray(
        np.asarray(ln_b, np.float32).reshape(1, 16).repeat(128, axis=0))
    pb = np.ascontiguousarray(
        np.asarray(proj_b, np.float32).reshape(1, 16).repeat(128, axis=0))
    return dict(c_bdqkv=bd_qkv, c_bdproj=bd_proj, c_h128=h128, c_ident=ident,
                c_taps=taps, c_cb=cb, c_lnw=lnw, c_lnb=lnb, c_pb=pb)


def kernel(L0_attn_x, L0_attn_qkv_w, L0_attn_conv_w0, L0_attn_conv_b0,
           L0_attn_conv_w1, L0_attn_conv_b1, L0_attn_proj_w, L0_attn_proj_b,
           L0_attn_ln_w, L0_attn_ln_b, L0_ffn_w_gate, L0_ffn_w_up,
           L0_ffn_w_down):
    x = np.ascontiguousarray(np.asarray(L0_attn_x, np.float32)[0])
    consts = _host_constants(np.asarray(L0_attn_qkv_w, np.float32),
                             np.asarray(L0_attn_proj_w, np.float32),
                             np.asarray(L0_attn_conv_w0, np.float32),
                             np.asarray(L0_attn_conv_b0, np.float32),
                             np.asarray(L0_attn_conv_w1, np.float32),
                             np.asarray(L0_attn_conv_b1, np.float32),
                             np.asarray(L0_attn_ln_w, np.float32),
                             np.asarray(L0_attn_ln_b, np.float32),
                             np.asarray(L0_attn_proj_b, np.float32))
    wg = np.ascontiguousarray(np.asarray(L0_ffn_w_gate, np.float32))
    wu = np.ascontiguousarray(np.asarray(L0_ffn_w_up, np.float32))
    wd = np.ascontiguousarray(np.asarray(L0_ffn_w_down, np.float32))

    nc = build_nc()
    in_maps = []
    for k in range(8):
        xres_k = np.ascontiguousarray(
            np.stack([x[c * 1024 + 128 * k: c * 1024 + 128 * k + 128]
                      for c in range(NT)]))
        m = {"xs": np.ascontiguousarray(x[2048 * k:2048 * (k + 1)]),
             "xres": xres_k, "w_g": wg, "w_u": wu, "w_d": wd}
        m.update(consts)
        in_maps.append(m)
    res = run_bass_kernel_spmd(nc, in_maps, core_ids=list(range(8)))
    outp = np.empty((16384, 2048), np.float32)
    for k in range(8):
        ok = res.results[k]["out"]
        for c in range(NT):
            outp[c * 1024 + 128 * k: c * 1024 + 128 * k + 128] = ok[c]
    return outp.reshape(1, 16384, 2048)
